# version 27
# baseline (speedup 1.0000x reference)
import numpy as np
import concourse.bass as bass
import concourse.mybir as mybir
from concourse.bass_utils import run_bass_kernel_spmd

F32 = mybir.dt.float32
BF16 = mybir.dt.bfloat16
AF = mybir.ActivationFunctionType
ALU = mybir.AluOpType
AX = mybir.AxisListType

D = 1024
NH = 16
DH = 64
B = 4
TFULL = 8192
NCORE = 8
CT = 4096
TT = 256
NTILE = CT // TT
WARM = 512
NWT = WARM // TT
C = 64
NCH = TT // C
NM = 8
EPS = 1e-5
NE = 16
NG = (TT * 8) // 512
_SS = 99
_TAPS = ()


def _slab_layout():
    off, o = {}, 0
    def put(name, w):
        nonlocal o
        off[name] = (o, w)
        o += w
    for nm in ("wq", "wk", "wv", "wg", "wo", "wm", "wo2"):
        put(nm, 8192)
    put("w1w2", NE * 1024)
    put("emb", 16)
    put("b2", 128)
    put("pv", 72)
    put("b1v", 64)
    put("al", 8)
    put("wmod", 8 * 6144)
    put("cq", 64)
    put("ck", 64)
    put("ck2", 64)
    put("tril", 256)
    put("i128", 128)
    return off, o


SLAB_OFF, SLAB_W = _slab_layout()


def _host_slab(inp):
    ws = np.zeros((128, SLAB_W), np.float32)
    def put(name, arr):
        o, w = SLAB_OFF[name]
        ws[:arr.shape[0], o:o + arr.shape[1]] = arr
    def fold(m):
        X = m.shape[1]
        return np.ascontiguousarray(m.reshape(8, 128, X).transpose(1, 0, 2).reshape(128, 8 * X))
    put("wq", fold(np.ascontiguousarray(np.asarray(inp["L0_attn_WQ"]).T)))
    put("wk", fold(np.ascontiguousarray(np.asarray(inp["L0_attn_WK"]).T)))
    put("wv", fold(np.ascontiguousarray(np.asarray(inp["L0_attn_WV"]).T)))
    put("wg", fold(np.ascontiguousarray(np.asarray(inp["L0_attn_Wg"]).T)))
    put("wo", fold(np.ascontiguousarray(np.asarray(inp["L0_attn_WO"]).T)))
    put("wm", fold(np.asarray(inp["L0_ffn_Wm"])))
    put("wo2", fold(np.asarray(inp["L0_ffn_Wo"])))
    w12 = np.zeros((128, NE * 1024), np.float32)
    W1 = np.asarray(inp["L0_ffn_W1"])
    W2 = np.asarray(inp["L0_ffn_W2"])
    for e in range(NE):
        w12[:, e * 1024:e * 1024 + 512] = W1[e]
        w12[:, e * 1024 + 512:(e + 1) * 1024] = \
            W2[e].reshape(4, 128, 128).transpose(1, 0, 2).reshape(128, 512)
    put("w1w2", w12)
    put("emb", np.asarray(inp["L0_ffn_expert_emb"]))
    put("b2", np.asarray(inp["L0_ffn_b2"]))
    pvh = np.zeros((128, 72), np.float32)
    for i, nm in enumerate(["L0_attn_bQ", "L0_attn_bK", "L0_attn_bV", "L0_attn_bg",
                            "L0_attn_bO", "L0_ffn_bm", "L0_ffn_bo",
                            "L0_attn_ln_gamma", "L0_attn_ln_beta"]):
        pvh[:, i * 8:(i + 1) * 8] = np.asarray(inp[nm]).reshape(8, 128).T
    put("pv", pvh)
    b1 = np.asarray(inp["L0_ffn_b1"])
    b1h = np.zeros((128, 64), np.float32)
    for e in range(NE):
        b1h[:, e * 4:(e + 1) * 4] = b1[e].reshape(4, 128).T
    put("b1v", b1h)
    al = np.asarray(inp["L0_attn_alpha_logit"])
    put("al", np.concatenate([al[0::2].T, al[1::2].T], axis=0))
    put("wmod", fold(np.asarray(inp["L0_wmod"])))
    put("cq", np.broadcast_to(np.arange(1, 65, dtype=np.float32), (128, 64)))
    put("ck", np.broadcast_to(-np.arange(1, 65, dtype=np.float32), (128, 64)))
    put("ck2", np.broadcast_to(63.0 - np.arange(64, dtype=np.float32), (128, 64)))
    tr = (np.arange(64)[:, None] <= np.arange(64)[None, :]).astype(np.float32)
    put("tril", np.tile(tr, (2, 4)))
    put("i128", np.eye(128, dtype=np.float32))
    return ws


class Prog:
    ENGS = ("tensor", "vector", "scalar", "sync", "gpsimd")
    E2S = {"tensor": "T", "vector": "V", "scalar": "S", "sync": "X", "gpsimd": "G"}

    def __init__(self):
        self.q = {e: [] for e in self.ENGS}
        self.cnt = {s: 0 for s in "TVSXG"}
        self.res = {}
        self.lastwait = {e: {} for e in self.ENGS}

    def group(self, eng, builders, reads=(), writes=(), dma=False):
        assert builders
        sem = self.E2S[eng]
        own_dma = eng in ("sync", "gpsimd")
        waits = {}
        def need(s, v):
            if s == sem and not own_dma:
                return
            if s == sem and own_dma and not dma:
                return
            waits[s] = max(waits.get(s, 0), v)
        for n in reads:
            e = self.res.get(n)
            if e and e["w"]:
                need(*e["w"])
        for n in writes:
            e = self.res.get(n)
            if e:
                if e["w"]:
                    need(*e["w"])
                for s, v in e["r"].items():
                    need(s, v)
        for s in sorted(waits):
            v = waits[s]
            lw = self.lastwait[eng]
            if lw.get(s, -1) >= v:
                continue
            lw[s] = v
            self.q[eng].append(("w", s, v))
        if dma:
            self.cnt[sem] += 16 * len(builders)
            for b in builders:
                self.q[eng].append(("d", b))
        else:
            self.cnt[sem] += 1
            for i, b in enumerate(builders):
                self.q[eng].append(("i", b) if i < len(builders) - 1 else ("f", b))
        endv = self.cnt[sem]
        for n in reads:
            e = self.res.setdefault(n, {"w": None, "r": {}})
            e["r"][sem] = endv
        for n in writes:
            self.res[n] = {"w": (sem, endv), "r": {}}
        return endv


def build_nc():
    nc = bass.Bass("TRN2", target_bir_lowering=False)

    xT = nc.dram_tensor("xT", [D, CT], F32, kind="ExternalInput").ap()
    xw = nc.dram_tensor("xw", [D, WARM], F32, kind="ExternalInput").ap()
    slab = nc.dram_tensor("wslab", [128, SLAB_W], F32, kind="ExternalInput").ap()
    cTd = nc.dram_tensor("cT", [128, 8], F32, kind="ExternalInput").ap()
    wcd = nc.dram_tensor("wcol", [128, 1], F32, kind="ExternalInput").ap()
    oT = nc.dram_tensor("oT", [D, CT], F32, kind="ExternalOutput").ap()

    A = nc.alloc_sbuf_tensor
    wbf = {nm: A("wbf_" + nm, [128, 8192], BF16)
           for nm in ("wq", "wk", "wv", "wg", "wo", "wm", "wo2")}
    embbf = A("embbf", [128, 16], BF16)
    b2bf = A("b2bf", [16, 128], BF16)
    i128b = A("i128b", [128, 128], BF16)
    i128f = A("i128f", [128, 128], F32)
    trilr = A("trilr", [128, 256], BF16)
    onescol = A("onescol", [128, 1], BF16)
    onesrow = A("onesrow", [1, 128], BF16)
    pv = A("pv", [128, 72], F32)
    b1v = A("b1v", [128, 64], F32)
    alv = A("alv", [128, 8], F32)
    lna = A("lna", [128, 8], F32)
    acl = A("acl", [128, 8], F32)
    modv = A("modv", [128, 48], F32)
    bog1 = A("bog1", [128, 8], F32)
    bog2 = A("bog2", [128, 8], F32)
    dq = A("dq", [128, 512], BF16)
    dk = A("dk", [128, 512], BF16)
    dk2t = A("dk2t", [128, 1024], BF16)
    bkd2 = A("bkd2", [128, 1024], BF16)
    bvrep = A("bvrep", [128, 1024], BF16)
    wcol = A("wcol_sb", [128, 1], F32)
    xsb = A("xsb", [128, 2048], F32)
    hbuf = A("hbuf", [128, 2048], BF16)
    xnb = A("xnb", [128, 2048], BF16)
    qb = A("qb", [128, 2048], BF16)
    kb = A("kb", [128, 2048], BF16)
    vd = A("vd", [128, 4096], BF16)
    kh = A("kh", [128, 2048], BF16)
    gbuf = A("gbuf", [128, 2048], BF16)
    atb = A("atb", [128, 512], BF16)
    ones512 = A("ones512", [1, 512], BF16)
    hst = A("hst", [128, 512], BF16)
    accg = A("accg", [128, 2048], BF16)
    hmid = A("hmid", [128, 2048], BF16)
    mrep = A("mrep", [128, 2048], BF16)
    combt = A("combt", [128, 1024], BF16)
    mrow = A("mrow", [1, 1024], BF16)
    mfm = A("mfm", [16, 2048], BF16)
    mtm = A("mtm", [128, 256], F32)
    wsb = A("wsb", [128, 256], F32)
    smx = A("smx", [128, 64], F32)
    sqb = A("sqb", [128, 512], BF16)
    stv = A("stv", [1, 768], F32)
    stvb = A("stvb", [1, 512], BF16)
    nsc = A("nsc", [128, 512], F32)
    webf = A("webf", [128, 2048], BF16)
    wstg = A("wstg", [128, 2048], F32)
    silc = A("silc", [128, 8], F32)
    epsc = A("epsc", [128, 1], F32)

    b1t = b1v[0:64, :].bitcast(BF16)
    pb = nc.alloc_psum_tensor("pb", [128, 4096], F32)

    def bank(i, rows=None, cols=None):
        r = rows if rows is not None else slice(0, 128)
        c = cols if cols is not None else slice(0, 512)
        return pb[r, i * 512 + c.start: i * 512 + c.stop]

    P = Prog()
    g = P.group
    WH = ("wstg0", "wstg1")
    tapd = {}

    def tap(name, ap, reads):
        if name not in _TAPS:
            return
        shp = list(ap.shape)
        t = nc.dram_tensor("tap_" + name, shp, F32, kind="ExternalOutput").ap()
        tapd[name] = True
        g("gpsimd", [lambda e: e.dma_start(out=t, in_=ap)], reads=reads, dma=True)

    def sl(name, csl=None):
        o, w = SLAB_OFF[name]
        if csl is None:
            csl = slice(0, w)
        return slab[:, o + csl.start:o + csl.stop]

    g("sync", [
        lambda e: e.dma_start(out=pv[:], in_=sl("pv")),
        lambda e: e.dma_start(out=b1v[:], in_=sl("b1v")),
        lambda e: e.dma_start(out=alv[:], in_=sl("al")),
        lambda e: e.dma_start(out=wstg[:, 0:16], in_=sl("emb")),
        lambda e: e.dma_start(out=wstg[:, 16:144], in_=sl("b2")),
        lambda e: e.dma_start(out=wstg[:, 144:400], in_=sl("tril")),
        lambda e: e.dma_start(out=wstg[:, 400:528], in_=sl("i128")),
        lambda e: e.dma_start(out=wstg[:, 536:600], in_=sl("ck")),
        lambda e: e.dma_start(out=wstg[:, 600:664], in_=sl("cq")),
        lambda e: e.dma_start(out=wstg[:, 664:728], in_=sl("ck2")),
        lambda e: e.dma_start(out=silc[:], in_=cTd),
        lambda e: e.dma_start(out=wcol[:], in_=wcd),
    ], writes=WH + ("pv", "b1v", "alv", "silc", "wcol"), dma=True)
    g("vector", [
        lambda e: e.tensor_copy(embbf[:], wstg[:, 0:16]),
        lambda e: e.tensor_copy(b2bf[:], wstg[0:16, 16:144]),
        lambda e: e.tensor_copy(trilr[:], wstg[:, 144:400]),
        lambda e: e.tensor_copy(i128b[:], wstg[:, 400:528]),
        lambda e: e.tensor_copy(i128f[:], wstg[:, 400:528]),
        lambda e: e.memset(onesrow[:], 1.0),
        lambda e: e.memset(onescol[:], 1.0),
        lambda e: e.memset(epsc[:], EPS),
        lambda e: e.memset(ones512[:], 1.0),
    ], reads=WH, writes=("embbf", "b2bf", "trilr", "i128b", "i128f", "onesrow",
                         "onescol", "ones512"))
    g("tensor", [lambda e: e.transpose(pb[0:64, 3584:3712], b1v[:], i128f[:])],
      reads=("b1v", "i128f"), writes=("p7",))
    g("vector", [lambda e: e.tensor_copy(b1t, pb[0:64, 3584:3712])],
      reads=("p7",), writes=("b1t", "b1v"))

    if _SS < 1: return _fin(nc, P)
    g("scalar", [
        lambda e: e.activation(lna[:], alv[:], AF.Sigmoid),
        lambda e: e.drain(),
        lambda e: e.activation(lna[:], lna[:], AF.Ln),
        lambda e: e.drain(),
        lambda e: e.activation(acl[:], lna[:], AF.Exp, scale=64.0),
    ], reads=("alv",), writes=("lna", "acl"))
    for src_off, dst, dnm in ((600, dq, "dq"), (536, dk, "dk")):
        g("vector", [lambda e, m=m, src_off=src_off: e.tensor_scalar(
            wstg[:, 1048 + m * 64:1048 + (m + 1) * 64],
            wstg[:, src_off:src_off + 64], lna[:, m:m + 1], None, ALU.mult)
            for m in range(NM)], reads=WH + ("lna",), writes=WH)
        g("scalar", [lambda e, dst=dst: e.activation(dst[:], wstg[:, 1048:1560], AF.Exp)],
          reads=WH, writes=(dnm,))
    if _SS < 2: return _fin(nc, P)
    g("vector", [lambda e, m=m: e.tensor_scalar(
        wstg[:, 1048 + m * 64:1048 + (m + 1) * 64],
        wstg[:, 664:728], lna[:, m:m + 1], None, ALU.mult)
        for m in range(NM)], reads=WH + ("lna",), writes=WH)
    g("tensor", [lambda e, h=h: e.transpose(
        pb[0:64, (h % 2) * 512 + (h // 2) * 64:(h % 2) * 512 + (h // 2) * 64 + 64],
        wstg[(h % 2) * 64:(h % 2) * 64 + 64,
             1048 + (h // 2) * 64:1048 + (h // 2) * 64 + 64],
        i128f[(h % 2) * 64:(h % 2) * 64 + 64, (h % 2) * 64:(h % 2) * 64 + 64])
        for h in range(16)],
      reads=WH + ("i128f",), writes=("p0", "p1"))
    g("scalar", [lambda e, hp=hp: e.activation(
        dk2t[0:64, hp * 64:1024].rearrange("p (a c) -> p a c", c=64)[:, 0:8:1, :]
        if False else
        dk2t[0:64, 0:1024].rearrange("p (a c) -> p a c", c=64)[:, hp::2, :],
        pb[0:64, hp * 512:hp * 512 + 512].rearrange("p (a c) -> p a c", c=64),
        AF.Exp) for hp in range(2)],
      reads=("p0", "p1"), writes=("dk2t",))
    g("gpsimd", [lambda e: e.dma_start(out=dk2t[64:128, :], in_=dk2t[0:64, :])],
      reads=("dk2t",), writes=("dk2t",), dma=True)

    if _SS < 3: return _fin(nc, P)
    for vi, dst in ((1, "bkd2"), (2, "bvrep")):
        g("tensor", [lambda e, kc=kc, vi=vi: e.transpose(
            pb[0:1, kc * 128:(kc + 1) * 128],
            pv[:, vi * 8 + kc:vi * 8 + kc + 1], i128f[:])
            for kc in range(8)],
          reads=("pv", "i128f", "stv"), writes=("p0", "p1"))
        for rr in range(2):
            g("vector", [lambda e, rr=rr: e.tensor_copy(
                stv[0:1, 0:512], pb[0:1, rr * 512:rr * 512 + 512])],
              reads=("p0", "p1"), writes=("stv",))
            g("gpsimd", [lambda e, rr=rr: e.dma_start(
                out=wstg[:, rr * 512:rr * 512 + 512],
                in_=stv[0:1, 0:512].unsqueeze(1).broadcast_to([1, 128, 512]))],
              reads=("stv",), writes=("wstg%d" % rr,), dma=True)
        if dst == "bkd2":
            g("vector", [
                lambda e: e.tensor_copy(bkd2[:], wstg[:, 0:1024]),
                lambda e: e.tensor_tensor(bkd2[:], bkd2[:], dk2t[:], ALU.mult),
            ], reads=WH + ("dk2t",), writes=("bkd2",))
        else:
            g("vector", [lambda e: e.tensor_copy(bvrep[:], wstg[:, 0:1024])],
              reads=WH, writes=("bvrep",))

    if _SS < 4: return _fin(nc, P)
    g("scalar", [lambda e: e.activation(silc[:], silc[:], AF.Silu)],
      reads=("silc",), writes=("silc",))
    wi = 0
    for dc in range(8):
        for s3 in range(6):
            lo = dc * 6144 + s3 * 1024
            hb_ = wi % 2
            g("sync", [lambda e, lo=lo, hb_=hb_: e.dma_start(
                out=wstg[:, hb_ * 1024:(hb_ + 1) * 1024], in_=sl("wmod", slice(lo, lo + 1024)))],
              writes=("wstg%d" % hb_,), dma=True)
            g("tensor", [lambda e, dc=dc, fi=fi, fc=s3 * 8 + fi, hb_=hb_: e.matmul(
                pb[:, 2048 + fc * 8 + dc:2048 + fc * 8 + dc + 1],
                wstg[:, hb_ * 1024 + fi * 128:hb_ * 1024 + (fi + 1) * 128],
                silc[:, dc:dc + 1], start=True, stop=True)
                for fi in range(8)],
              reads=("wstg%d" % hb_, "silc"), writes=("p4",))
            wi += 1
    g("vector", [
        lambda e: e.tensor_reduce(
            modv[:].rearrange("p (a c) -> p a c", c=1),
            pb[:, 2048:2432].rearrange("p (a c) -> p a c", c=8), AX.X, ALU.add),
        lambda e: e.drain(),
        lambda e: e.tensor_scalar(modv[:, 8:16], modv[:, 8:16], 1.0, None, ALU.add),
        lambda e: e.tensor_scalar(modv[:, 32:40], modv[:, 32:40], 1.0, None, ALU.add),
    ], reads=("p4",), writes=("modv",))
    clops = []
    for j, cg in enumerate((16, 40)):
        clops.append(lambda e, cg=cg, j=j: e.tensor_scalar(
            nsc[:, j * 8:j * 8 + 8], modv[:, cg:cg + 8], 0.25, None, ALU.max))
    for j, cg in enumerate((16, 40)):
        clops.append(lambda e, cg=cg, j=j: e.tensor_scalar(
            nsc[:, 16 + j * 8:24 + j * 8], modv[:, cg:cg + 8], -0.25, None, ALU.min))
    for j, cg in enumerate((16, 40)):
        clops.append(lambda e, cg=cg, j=j: e.tensor_scalar(
            nsc[:, 32 + j * 8:40 + j * 8], modv[:, cg:cg + 8], 0.0, None, ALU.is_ge))
    for j in range(2):
        clops.append(lambda e, j=j: e.tensor_tensor(
            nsc[:, j * 8:j * 8 + 8], nsc[:, j * 8:j * 8 + 8],
            nsc[:, 16 + j * 8:24 + j * 8], ALU.subtract))
    for j in range(2):
        clops.append(lambda e, j=j: e.tensor_tensor(
            nsc[:, j * 8:j * 8 + 8], nsc[:, 32 + j * 8:40 + j * 8],
            nsc[:, j * 8:j * 8 + 8], ALU.mult))
    for j, cg in enumerate((16, 40)):
        clops.append(lambda e, cg=cg, j=j: e.tensor_tensor(
            modv[:, cg:cg + 8], nsc[:, j * 8:j * 8 + 8],
            nsc[:, 16 + j * 8:24 + j * 8], ALU.add))
    g("vector", clops, reads=("modv",), writes=("modv",))
    g("vector", [
        lambda e: e.tensor_tensor(bog1[:], pv[:, 32:40], modv[:, 16:24], ALU.mult),
        lambda e: e.tensor_tensor(bog2[:], pv[:, 48:56], modv[:, 40:48], ALU.mult),
        lambda e: e.memset(hst[:], 0.0),
    ], reads=("pv", "modv"), writes=("bog1", "bog2", "hst"))

    if _SS < 5: return _fin(nc, P)
    pi = 0
    for nm in ("wq", "wk", "wv", "wg", "wo", "wm", "wo2"):
        for part in range(8):
            lo = part * 1024
            hb_ = pi % 2
            g("sync", [lambda e, nm=nm, lo=lo, hb_=hb_: e.dma_start(
                out=wstg[:, hb_ * 1024:(hb_ + 1) * 1024], in_=sl(nm, slice(lo, lo + 1024)))],
              writes=("wstg%d" % hb_,), dma=True)
            eng = "vector" if pi % 4 < 2 else "scalar"
            if eng == "vector":
                g("vector", [lambda e, nm=nm, lo=lo, hb_=hb_: e.tensor_copy(
                    wbf[nm][:, lo:lo + 1024], wstg[:, hb_ * 1024:(hb_ + 1) * 1024])],
                  reads=("wstg%d" % hb_,), writes=(nm,))
            else:
                g("scalar", [lambda e, nm=nm, lo=lo, hb_=hb_: e.copy(
                    wbf[nm][:, lo:lo + 1024], wstg[:, hb_ * 1024:(hb_ + 1) * 1024])],
                  reads=("wstg%d" % hb_,), writes=(nm,))
            pi += 1

    st_mu, st_m2, st_r = slice(0, 256), slice(256, 512), slice(512, 768)

    def csl(k):
        return slice(k * 256, (k + 1) * 256)

    def pvc(vi, k):
        return pv[:, vi * 8 + k: vi * 8 + k + 1]

    def mvc(blk, k):
        return modv[:, blk * 8 + k: blk * 8 + k + 1]

    def ln_phase(src, sname, dst, dname, a_col, s_col, f32src):
        for k in range(8):
            sops = []
            if f32src:
                sops.append(lambda e, k=k: e.copy(hbuf[:, csl(k)], src[:, csl(k)]))
            sops.append(lambda e, k=k: e.activation(
                sqb[:, (k % 2) * 256:(k % 2) * 256 + 256], src[:, csl(k)], AF.Square))
            wr = (("hb%d" % k,) if f32src else ()) + ("sq%d" % (k % 2),)
            g("scalar", sops, reads=("%s%d" % (sname, k),), writes=wr)
            stat = hbuf if f32src else src
            statn = ("hb%d" % k) if f32src else ("%s%d" % (sname, k))
            g("tensor", [
                lambda e, k=k, stat=stat: e.matmul(
                    pb[0:1, 2560:2816], onescol[:], stat[:, csl(k)],
                    start=(k == 0), stop=(k == 7)),
                lambda e, k=k: e.matmul(
                    pb[0:1, 3072:3328], onescol[:],
                    sqb[:, (k % 2) * 256:(k % 2) * 256 + 256],
                    start=(k == 0), stop=(k == 7)),
            ], reads=(statn, "sq%d" % (k % 2), "onescol"), writes=("p5", "p6"))
        g("vector", [
            lambda e: e.tensor_scalar(stv[:, st_mu], pb[0:1, 2560:2816], 1.0 / D, None, ALU.mult),
            lambda e: e.tensor_scalar(stv[:, st_m2], pb[0:1, 3072:3328], 1.0 / D, None, ALU.mult),
            lambda e: e.tensor_tensor(stv[:, st_r], stv[:, st_mu], stv[:, st_mu], ALU.mult),
            lambda e: e.drain(),
            lambda e: e.tensor_tensor(stv[:, st_m2], stv[:, st_m2], stv[:, st_r], ALU.subtract),
        ], reads=("p5", "p6"), writes=("stv",))
        g("scalar", [
            lambda e: e.activation(stv[:, st_r], stv[:, st_m2], AF.Ln, bias=epsc[0:1, :]),
            lambda e: e.drain(),
            lambda e: e.activation(stv[:, st_r], stv[:, st_r], AF.Exp, scale=-0.5),
        ], reads=("stv",), writes=("stv",))
        g("vector", [
            lambda e: e.tensor_copy(stvb[:, 0:256], stv[:, st_r]),
            lambda e: e.tensor_tensor(stvb[:, 256:512], stv[:, st_mu], stv[:, st_r], ALU.mult),
        ], reads=("stv",), writes=("stvb",))
        g("tensor", [
            lambda e: e.matmul(pb[:, 2048:2304], onesrow[:], stvb[:, 0:256],
                               start=True, stop=True),
            lambda e: e.matmul(pb[:, 2304:2560], onesrow[:], stvb[:, 256:512],
                               start=True, stop=True),
        ], reads=("stvb", "onesrow"), writes=("p4",))
        nops = []
        for k in range(8):
            nops.append(lambda e, k=k: e.scalar_tensor_tensor(
                nsc[:, 0:256], src[:, csl(k)], a_col(k), pb[:, 2048:2304],
                ALU.mult, ALU.mult))
            nops.append(lambda e, k=k: e.scalar_tensor_tensor(
                nsc[:, 256:512], pb[:, 2304:2560], a_col(k), nsc[:, 0:256],
                ALU.mult, ALU.subtract))
            nops.append(lambda e, k=k: e.tensor_scalar(
                dst[:, csl(k)], nsc[:, 256:512], -1.0, s_col(k), ALU.mult, ALU.add))
        g("vector", nops,
          reads=tuple("%s%d" % (sname, k) for k in range(8)) + ("p4", "modv", "pv"),
          writes=tuple("%s%d" % (dname, k) for k in range(8)))

    XN = tuple("xn%d" % k for k in range(8))

    def proj_fm(wname, evac):
        for mc in range(8):
            bk = mc % 2
            g("tensor", [lambda e, mc=mc, kc=kc, bk=bk: e.matmul(
                bank(bk, cols=slice(0, 256)),
                wbf[wname][:, kc * 1024 + mc * 128: kc * 1024 + (mc + 1) * 128],
                xnb[:, csl(kc)], start=(kc == 0), stop=False) for kc in range(4)],
              reads=XN[:4] + (wname,), writes=("p%d" % bk,))
            g("tensor", [lambda e, mc=mc, kc=kc, bk=bk: e.matmul(
                bank(bk, cols=slice(0, 256)),
                wbf[wname][:, kc * 1024 + mc * 128: kc * 1024 + (mc + 1) * 128],
                xnb[:, csl(kc)], start=False, stop=(kc == 7)) for kc in range(4, 8)],
              reads=XN[4:] + (wname,), writes=("p%d" % bk,))
            evac(mc, bk)

    qrep = lambda mc: dq[:, mc * 64:(mc + 1) * 64].unsqueeze(1).broadcast_to([128, 4, 64])
    krep = lambda mc: dk[:, mc * 64:(mc + 1) * 64].unsqueeze(1).broadcast_to([128, 4, 64])

    def run_tile(warm, xsrc, t0, ti=-1):
        g("sync", [lambda e: e.dma_start(
            out=xsb[:].rearrange("p (k t) -> p k t", t=256),
            in_=xsrc[:, t0:t0 + 256].rearrange("(k p) t -> p k t", p=128))],
          writes=tuple("xsb%d" % k for k in range(8)), dma=True)

        ln_phase(xsb, "xsb", hbuf, "hb",
                 a_col=lambda k: mvc(1, k), s_col=lambda k: mvc(0, k), f32src=True)
        if ti == 0:
            tap("h", hbuf[:], tuple("hb%d" % k for k in range(8)))
        ln_phase(hbuf, "hb", xnb, "xn",
                 a_col=lambda k: pvc(7, k), s_col=lambda k: pvc(8, k), f32src=False)
        if ti == 0:
            tap("xn", xnb[:], XN)

        for tg in range(2):
            for wname in ("wk", "wv"):
                bb = 2 if wname == "wk" else 4
                g("tensor", [lambda e, kc=kc, nh_=nh_, tg=tg, wname=wname, bb=bb: e.matmul(
                    bank(bb + nh_),
                    xnb[:, kc * 256 + tg * 128: kc * 256 + (tg + 1) * 128],
                    wbf[wname][:, kc * 1024 + nh_ * 512: kc * 1024 + (nh_ + 1) * 512],
                    start=(kc == 0), stop=(kc == 7))
                    for kc in range(8) for nh_ in range(2)],
                  reads=XN + (wname,), writes=("p%d" % bb, "p%d" % (bb + 1)))
                if wname == "wk":
                    g("vector", [
                        lambda e, tg=tg: e.tensor_tensor(
                            kh[:, tg * 1024:(tg + 1) * 1024], pb[:, 1024:2048],
                            dk2t[:], ALU.mult),
                        lambda e, tg=tg: e.tensor_tensor(
                            kh[:, tg * 1024:(tg + 1) * 1024],
                            kh[:, tg * 1024:(tg + 1) * 1024], bkd2[:], ALU.add),
                    ], reads=("p2", "p3", "dk2t", "bkd2"), writes=("kh",))
                else:
                    c0, c1 = 2 * tg, 2 * tg + 1
                    g("vector", [
                        lambda e, c0=c0: e.tensor_tensor(
                            vd[0:64, c0 * 1024:(c0 + 1) * 1024], pb[0:64, 2048:3072],
                            bvrep[0:64, :], ALU.add),
                        lambda e, c1=c1: e.tensor_tensor(
                            vd[64:128, c1 * 1024:(c1 + 1) * 1024], pb[64:128, 2048:3072],
                            bvrep[64:128, :], ALU.add),
                    ], reads=("p4", "p5", "bvrep"), writes=("vd",))
                    g("gpsimd", [
                        lambda e, c0=c0: e.dma_start(
                            out=vd[64:128, c0 * 1024:(c0 + 1) * 1024],
                            in_=vd[0:64, c0 * 1024:(c0 + 1) * 1024]),
                        lambda e, c1=c1: e.dma_start(
                            out=vd[0:64, c1 * 1024:(c1 + 1) * 1024],
                            in_=vd[64:128, c1 * 1024:(c1 + 1) * 1024]),
                    ], reads=("vd",), writes=("vd",), dma=True)

        if not warm:
            def evq(mc, bk):
                g("vector", [lambda e, mc=mc, bk=bk: e.scalar_tensor_tensor(
                    qb[:, csl(mc)].rearrange("p (a c) -> p a c", c=64),
                    bank(bk, cols=slice(0, 256)).rearrange("p (a c) -> p a c", c=64),
                    pvc(0, mc), qrep(mc), ALU.add, ALU.mult)],
                  reads=("p%d" % bk, "pv", "dq"), writes=("qb%d" % mc,))
            proj_fm("wq", evq)

            def evk(mc, bk):
                g("vector", [lambda e, mc=mc, bk=bk: e.scalar_tensor_tensor(
                    kb[:, csl(mc)].rearrange("p (a c) -> p a c", c=64),
                    bank(bk, cols=slice(0, 256)).rearrange("p (a c) -> p a c", c=64),
                    pvc(1, mc), krep(mc), ALU.add, ALU.mult)],
                  reads=("p%d" % bk, "pv", "dk"), writes=("kb%d" % mc,))
            proj_fm("wk", evk)

            def evg(mc, bk):
                g("scalar", [lambda e, mc=mc, bk=bk: e.activation(
                    gbuf[:, csl(mc)], bank(bk, cols=slice(0, 256)), AF.Silu, bias=pvc(3, mc))],
                  reads=("p%d" % bk, "pv"),
                  writes=("gb%d" % mc,) + (("b1s1",) if mc < 2 else ()))
            proj_fm("wg", evg)

        if ti == 0 and not warm:
            tap("q", qb[:], tuple("qb%d" % k for k in range(8)))
            tap("kt", kb[:], tuple("kb%d" % k for k in range(8)))
            tap("gpre", gbuf[:], tuple("gb%d" % k for k in range(8)))
            tap("kh", kh[:], ("kh",))
            tap("vd", vd[:], ("vd",))
            tap("hst0", hst[:], ("hst",))
        def scan_S(m):
            mo = (m % 2) * 256
            g("tensor", [lambda e, c=c, hp=hp, m=m, mo=mo: e.matmul(
                bank(2 + hp, slice(hp * 64, hp * 64 + 64),
                     slice(mo + c * 64, mo + (c + 1) * 64)),
                kb[hp * 64:hp * 64 + 64, m * 256 + c * 64: m * 256 + (c + 1) * 64],
                qb[hp * 64:hp * 64 + 64, m * 256 + c * 64: m * 256 + (c + 1) * 64],
                start=True, stop=True)
                for c in range(NCH) for hp in range(2)],
              reads=("kb%d" % m, "qb%d" % m), writes=("p2", "p3"))

        def scan_at(m):
            mo = (m % 2) * 256
            g("vector", [
                lambda e, mo=mo: e.tensor_tensor(
                    atb[0:64, mo:mo + 256], pb[0:64, 1024 + mo:1024 + mo + 256],
                    trilr[0:64, :], ALU.mult),
                lambda e, mo=mo: e.tensor_tensor(
                    atb[64:128, mo:mo + 256], pb[64:128, 1536 + mo:1536 + mo + 256],
                    trilr[64:128, :], ALU.mult),
            ], reads=("p2", "p3", "trilr"), writes=("atb%d" % (m % 2), "b1s0"))

        def scan_c(m, c, warm):
            mo = (m % 2) * 256
            mo2 = (m % 2) * 128
            cp = c % 2
            tmm = []
            if not warm:
                for hp in range(2):
                    tmm.append(lambda e, hp=hp, m=m, c=c, mo=mo: e.matmul(
                        bank(4 + hp, slice(hp * 64, hp * 64 + 64),
                             slice(mo + c * 64, mo + (c + 1) * 64)),
                        hst[hp * 64:hp * 64 + 64, m * 64:(m + 1) * 64],
                        qb[hp * 64:hp * 64 + 64, m * 256 + c * 64: m * 256 + (c + 1) * 64],
                        start=True, stop=False))
                for hp in range(2):
                    h = 2 * m + hp
                    tmm.append(lambda e, hp=hp, h=h, c=c, mo=mo: e.matmul(
                        bank(4 + hp, slice(hp * 64, hp * 64 + 64),
                             slice(mo + c * 64, mo + (c + 1) * 64)),
                        vd[hp * 64:hp * 64 + 64, c * 1024 + h * 64: c * 1024 + (h + 1) * 64],
                        atb[hp * 64:hp * 64 + 64, mo + c * 64:mo + (c + 1) * 64],
                        start=False, stop=True))
            for hp in range(2):
                h = 2 * m + hp
                tmm.append(lambda e, hp=hp, h=h, c=c, cp=cp, mo2=mo2: e.matmul(
                    bank(6 + cp, slice(hp * 64, hp * 64 + 64),
                         slice(mo2 + (c // 2) * 64, mo2 + (c // 2) * 64 + 64)),
                    kh[cp * 64:cp * 64 + 64, (c // 2) * 1024 + h * 64: (c // 2) * 1024 + (h + 1) * 64],
                    vd[cp * 64:cp * 64 + 64, c * 1024 + h * 64: c * 1024 + (h + 1) * 64],
                    start=True, stop=True))
            rd = ("hst", "atb%d" % (m % 2), "vd", "kh", "qb%d" % m)
            wr = (("p4", "p5") if not warm else ()) + ("p%d" % (6 + cp),)
            g("tensor", tmm, reads=rd, writes=wr)

        def scan_h(m, c):
            mo2 = (m % 2) * 128
            cp = c % 2
            hops = []
            for hp in range(2):
                hops.append(lambda e, hp=hp, m=m, c=c, cp=cp, mo2=mo2: e.scalar_tensor_tensor(
                    hst[hp * 64:hp * 64 + 64, m * 64:(m + 1) * 64],
                    hst[hp * 64:hp * 64 + 64, m * 64:(m + 1) * 64],
                    acl[hp * 64:hp * 64 + 64, m:m + 1],
                    bank(6 + cp, slice(hp * 64, hp * 64 + 64),
                         slice(mo2 + (c // 2) * 64, mo2 + (c // 2) * 64 + 64)),
                    ALU.mult, ALU.add))
            g("vector", hops, reads=("p%d" % (6 + cp), "acl"), writes=("hst",))

        def scan_og(m):
            mo = (m % 2) * 256
            g("vector", [
                lambda e, m=m, mo=mo: e.tensor_tensor(
                    gbuf[0:64, m * 256:(m + 1) * 256], pb[0:64, 2048 + mo:2048 + mo + 256],
                    gbuf[0:64, m * 256:(m + 1) * 256], ALU.mult),
                lambda e, m=m, mo=mo: e.tensor_tensor(
                    gbuf[64:128, m * 256:(m + 1) * 256], pb[64:128, 2560 + mo:2560 + mo + 256],
                    gbuf[64:128, m * 256:(m + 1) * 256], ALU.mult),
            ], reads=("p4", "p5", "gb%d" % m), writes=("gb%d" % m,))

        if not warm:
            scan_S(0)
            scan_S(1)
        for mp in range(0, NM, 2):
            if not warm:
                scan_at(mp)
                scan_at(mp + 1)
            for c in range(NCH):
                scan_c(mp, c, warm)
                scan_c(mp + 1, c, warm)
                if not warm and mp + 2 < NM:
                    if c == 1:
                        scan_S(mp + 2)
                    elif c == 2:
                        scan_S(mp + 3)
                scan_h(mp, c)
                scan_h(mp + 1, c)
            if not warm:
                scan_og(mp)
                scan_og(mp + 1)

        if warm:
            return
        if ti == 0:
            tap("og", gbuf[:], tuple("gb%d" % k for k in range(8)))
            tap("hst1", hst[:], ("hst",))

        for mc in range(8):
            bk = mc % 2
            mm = [lambda e, mc=mc, kc=kc, bk=bk: e.matmul(
                bank(bk, cols=slice(0, 256)),
                wbf["wo"][:, kc * 1024 + mc * 128: kc * 1024 + (mc + 1) * 128],
                gbuf[:, csl(kc)], start=(kc == 0), stop=False) for kc in range(8)]
            mm.append(lambda e, mc=mc, bk=bk: e.matmul(
                bank(bk, cols=slice(0, 256)), i128b[:], hbuf[:, csl(mc)],
                start=False, stop=True))
            g("tensor", mm,
              reads=tuple("gb%d" % k for k in range(8)) +
                    tuple("hb%d" % k for k in range(8)) + ("wo", "i128b"),
              writes=("p%d" % bk,))
            g("vector", [
                lambda e, mc=mc, bk=bk: e.tensor_scalar(
                    nsc[:, 0:256], bank(bk, cols=slice(0, 256)), mvc(2, mc),
                    bog1[:, mc:mc + 1], ALU.mult, ALU.add),
                lambda e, mc=mc: e.tensor_tensor(
                    xsb[:, csl(mc)], xsb[:, csl(mc)], nsc[:, 0:256], ALU.add),
            ], reads=("p%d" % bk, "modv", "bog1", "xsb%d" % mc), writes=("xsb%d" % mc,))

        if ti == 0:
            tap("x1", xsb[:], tuple("xsb%d" % k for k in range(8)))
        ln_phase(xsb, "xsb", xnb, "xn",
                 a_col=lambda k: mvc(4, k), s_col=lambda k: mvc(3, k), f32src=True)

        def evm(mc, bk):
            g("vector", [lambda e, mc=mc, bk=bk: e.tensor_scalar(
                kb[:, csl(mc)], bank(bk, cols=slice(0, 256)), pvc(5, mc), None, ALU.add)],
              reads=("p%d" % bk, "pv"), writes=("kb%d" % mc,))
        proj_fm("wm", evm)

        if ti == 0:
            tap("tb", kb[:], tuple("kb%d" % k for k in range(8)))
        NU = NE * NG
        POOL2 = ((0, 1), (4, 5))

        def mrep_issue(ei, hh):
            g("gpsimd", [lambda e, ei=ei, hh=hh: e.dma_start(
                out=mrow[:], in_=mfm[ei:ei + 1, hh * 1024:(hh + 1) * 1024])],
              reads=("mfm",), writes=("mrow",), dma=True)
            g("tensor", [
                lambda e: e.matmul(pb[:, 3072:3584], onesrow[:], mrow[0:1, 0:512],
                                   start=True, stop=True),
                lambda e: e.matmul(pb[:, 3584:4096], onesrow[:], mrow[0:1, 512:1024],
                                   start=True, stop=True),
            ], reads=("mrow", "onesrow"), writes=("p6", "p7"))
            g("vector", [
                lambda e, hh=hh: e.tensor_copy(
                    mrep[:, hh * 1024:hh * 1024 + 512], pb[:, 3072:3584]),
                lambda e, hh=hh: e.tensor_copy(
                    mrep[:, hh * 1024 + 512:hh * 1024 + 1024], pb[:, 3584:4096]),
            ], reads=("p6", "p7"), writes=("mrep%d" % hh,))

        def bias_fetch(ei):
            eb = ei % 2
            dst = atb if eb == 0 else gbuf
            wr = ("b1s0", "atb0", "atb1") if eb == 0 else ("b1s1", "gb0", "gb1")
            g("gpsimd", [lambda e, ei=ei, dst=dst: e.dma_start(
                out=dst[0:1, 0:512].rearrange("a (r c) -> a r c", c=128),
                in_=b1t[4 * ei:4 * ei + 4, :])],
              reads=("b1t",), writes=wr, dma=True)

        def wfetch(ei):
            eb = ei % 2
            g("sync", [lambda e, ei=ei, eb=eb: e.dma_start(
                out=wstg[:, eb * 1024:(eb + 1) * 1024],
                in_=sl("w1w2", slice(ei * 1024, (ei + 1) * 1024)))],
              writes=("wstg%d" % eb,), dma=True)
            g("vector", [lambda e, eb=eb: e.tensor_copy(
                webf[:, eb * 1024:(eb + 1) * 1024], wstg[:, eb * 1024:(eb + 1) * 1024])],
              reads=("wstg%d" % eb,), writes=("webf%d" % eb,))

        def ffn_a(j):
            u, q = j // 2, j % 2
            ei, gi = u // NG, u % NG
            eb = ei % 2
            if gi == 1 and q == 0 and ei + 1 < NE:
                wfetch(ei + 1)
                bias_fetch(ei + 1)
            if gi == 1 and q == 0 and ei >= 1:
                mrep_issue(ei, 1)
            if gi == 3 and q == 0 and ei + 1 < NE:
                mrep_issue(ei + 1, 0)
            b0, b1 = POOL2[j % 2]
            wof = eb * 1024 + q * 256
            tsl = slice(gi * 512, (gi + 1) * 512)
            kread = ("kb%d" % (2 * gi), "kb%d" % (2 * gi + 1))
            eb = ei % 2
            bsl = atb if eb == 0 else gbuf
            bc = 2 * q * 128
            g("tensor", [
                lambda e, wof=wof, tsl=tsl, b0=b0: e.matmul(
                    bank(b0), webf[:, wof:wof + 128], kb[:, tsl], start=True, stop=False),
                lambda e, wof=wof, tsl=tsl, b1=b1: e.matmul(
                    bank(b1), webf[:, wof + 128:wof + 256], kb[:, tsl], start=True, stop=False),
                lambda e, b0=b0, bc=bc, bsl=bsl: e.matmul(
                    bank(b0), bsl[0:1, bc:bc + 128], ones512[:], start=False, stop=True),
                lambda e, b1=b1, bc=bc, bsl=bsl: e.matmul(
                    bank(b1), bsl[0:1, bc + 128:bc + 256], ones512[:], start=False, stop=True),
            ], reads=kread + ("webf%d" % (ei % 2), "b1s%d" % eb, "ones512"),
              writes=("p%d" % b0, "p%d" % b1))

        def ffn_b(j):
            u, q = j // 2, j % 2
            ei = u // NG
            b0, b1 = POOL2[j % 2]
            sl0 = (j % 2) * 1024
            g("scalar", [
                lambda e, b0=b0, sl0=sl0: e.activation(
                    hmid[:, sl0:sl0 + 512], bank(b0), AF.Gelu),
                lambda e, b1=b1, sl0=sl0: e.activation(
                    hmid[:, sl0 + 512:sl0 + 1024], bank(b1), AF.Gelu),
            ], reads=("p%d" % b0, "p%d" % b1),
              writes=("hm%d" % (2 * q), "hm%d" % (2 * q + 1)))

        def ffn_c(j):
            u, q = j // 2, j % 2
            ei = u // NG
            eo = 2 + (u % 2)
            eb = ei % 2
            wof = eb * 1024 + 512 + q * 256
            sl0 = (j % 2) * 1024
            g("tensor", [
                lambda e, wof=wof, eo=eo, sl0=sl0, q=q: e.matmul(
                    bank(eo), webf[:, wof:wof + 128], hmid[:, sl0:sl0 + 512],
                    start=(q == 0), stop=False),
                lambda e, wof=wof, eo=eo, sl0=sl0, q=q: e.matmul(
                    bank(eo), webf[:, wof + 128:wof + 256], hmid[:, sl0 + 512:sl0 + 1024],
                    start=False, stop=(q == 1)),
            ], reads=("webf%d" % eb, "hm%d" % (2 * q), "hm%d" % (2 * q + 1)),
              writes=("p%d" % eo,))

        def ffn_Fm(u):
            gi = u % NG
            eo = 2 + (u % 2)
            hh, off = gi // 2, (gi % 2) * 512
            cs = (u % 2) * 512
            g("vector", [lambda e, eo=eo, hh=hh, off=off, cs=cs: e.tensor_tensor(
                combt[:, cs:cs + 512], bank(eo),
                mrep[:, hh * 1024 + off:hh * 1024 + off + 512], ALU.mult)],
              reads=("p%d" % eo, "mrep%d" % hh), writes=("combt%d" % (u % 2),))

        def ffn_Fa(u):
            gi = u % NG
            cs = (u % 2) * 512
            g("vector", [lambda e, gi=gi, cs=cs: e.tensor_tensor(
                accg[:, gi * 512:(gi + 1) * 512], accg[:, gi * 512:(gi + 1) * 512],
                combt[:, cs:cs + 512], ALU.add)],
              reads=("combt%d" % (u % 2), "acc%d" % gi), writes=("acc%d" % gi,))

        def acc_init():
            for gi in range(NG):
                g("tensor", [lambda e, gi=gi: e.matmul(
                    pb[:, 3072:3584], b2bf[:], mfm[:, gi * 512:(gi + 1) * 512],
                    start=True, stop=True)], reads=("mfm", "b2bf"), writes=("p6",))
                g("vector", [lambda e, gi=gi: e.tensor_copy(
                    accg[:, gi * 512:(gi + 1) * 512], pb[:, 3072:3584])],
                  reads=("p6",), writes=("acc%d" % gi,))

        g("tensor", [lambda e, sc=sc: e.matmul(
            pb[:, 3072 + sc * 16:3072 + (sc + 1) * 16],
            kb[:, sc * 128:(sc + 1) * 128], embbf[:], start=True, stop=True)
            for sc in range(16)],
          reads=tuple("kb%d" % k for k in range(8)) + ("embbf",), writes=("p6",))
        wfetch(0)
        bias_fetch(0)
        ffn_a(0)
        ffn_b(0)
        ffn_a(1)
        ffn_b(1)
        ffn_a(2)
        ffn_c(0)
        ffn_b(2)
        ffn_a(3)
        ffn_c(1)
        ffn_b(3)
        ffn_a(4)
        g("vector", [lambda e: e.tensor_reduce(
            smx[:, 0:16].rearrange("p (a c) -> p a c", c=1),
            pb[:, 3072:3328].rearrange("p (a c) -> p a c", c=16), AX.X, ALU.max),
            lambda e: e.drain()],
          reads=("p6",), writes=("smx",))
        g("vector", [lambda e, sc=sc: e.tensor_scalar(
            wsb[:, sc * 16:(sc + 1) * 16], pb[:, 3072 + sc * 16:3072 + (sc + 1) * 16],
            smx[:, sc:sc + 1], None, ALU.subtract) for sc in range(16)],
          reads=("p6", "smx"), writes=("wsbf",))
        g("scalar", [lambda e: e.activation(wsb[:], wsb[:], AF.Exp)],
          reads=("wsbf",), writes=("wsbf",))
        g("vector", [
            lambda e: e.tensor_reduce(
                smx[:, 16:32].rearrange("p (a c) -> p a c", c=1),
                wsb[:].rearrange("p (a c) -> p a c", c=16), AX.X, ALU.add),
            lambda e: e.tensor_reduce(
                smx[:, 0:16].rearrange("p (a c) -> p a c", c=1),
                wsb[:].rearrange("p (a c) -> p a c", c=16), AX.X, ALU.max),
            lambda e: e.reciprocal(smx[:, 32:48], smx[:, 16:32]),
            lambda e: e.drain(),
        ], reads=("wsbf",), writes=("smx",))
        mops = []
        for sc in range(16):
            mops.append(lambda e, sc=sc: e.tensor_scalar(
                mtm[:, sc * 16:(sc + 1) * 16], wsb[:, sc * 16:(sc + 1) * 16],
                smx[:, sc:sc + 1], 0.0, ALU.subtract, ALU.is_lt))
        for sc in range(16):
            mops.append(lambda e, sc=sc: e.tensor_tensor(
                mtm[:, sc * 16:(sc + 1) * 16], wsb[:, sc * 16:(sc + 1) * 16],
                mtm[:, sc * 16:(sc + 1) * 16], ALU.mult))
        g("vector", mops, reads=("wsbf", "smx"), writes=("mtm",))
        g("vector", [lambda e: e.drain(), lambda e: e.tensor_reduce(
            smx[:, 48:64].rearrange("p (a c) -> p a c", c=1),
            mtm[:].rearrange("p (a c) -> p a c", c=16), AX.X, ALU.max),
            lambda e: e.drain()],
          reads=("mtm",), writes=("smx",))
        mops2 = []
        for sc in range(16):
            mops2.append(lambda e, sc=sc: e.tensor_scalar(
                mtm[:, sc * 16:(sc + 1) * 16], wsb[:, sc * 16:(sc + 1) * 16],
                smx[:, 48 + sc:49 + sc], 0.0, ALU.subtract, ALU.is_ge))
        for sc in range(16):
            mops2.append(lambda e, sc=sc: e.scalar_tensor_tensor(
                mtm[:, sc * 16:(sc + 1) * 16], mtm[:, sc * 16:(sc + 1) * 16],
                smx[:, 32 + sc:33 + sc], wsb[:, sc * 16:(sc + 1) * 16],
                ALU.mult, ALU.mult))
        g("vector", mops2, reads=("wsbf", "smx"), writes=("mtm",))
        def mtrans(rnd):
            g("tensor", [lambda e, sc=sc: e.transpose(
                pb[0:16, 3584 + (sc % 4) * 128:3584 + (sc % 4) * 128 + 128],
                mtm[:, sc * 16:(sc + 1) * 16], i128f[:])
                for sc in range(rnd * 4, rnd * 4 + 4)],
              reads=("mtm", "i128f"), writes=("p7",))
            g("vector", [lambda e, rnd=rnd: e.tensor_copy(
                mfm[:, rnd * 512:(rnd + 1) * 512], pb[0:16, 3584:4096])],
              reads=("p7",), writes=("mfm",))

        if ti == 0:
            tap("mtm", mtm[:], ("mtm",))
            tap("mfm", mfm[:], ("mfm",))
        NJ = 2 * NU
        mtrans(0)
        mtrans(1)
        mtrans(2)
        mtrans(3)
        mrep_issue(0, 0)
        mrep_issue(0, 1)
        acc_init()
        ffn_c(2)
        ffn_b(4)
        ffn_Fm(0)
        for j in range(5, NJ):
            ffn_a(j)
            ffn_c(j - 2)
            if (j - 2) % 2 == 1:
                u = (j - 2) // 2
                if u >= 1:
                    ffn_Fm(u)
                if u >= 1:
                    ffn_Fa(u - 1)
            ffn_b(j)
        ffn_c(NJ - 2)
        ffn_c(NJ - 1)
        ffn_Fm(NU - 1)
        ffn_Fa(NU - 2)
        ffn_Fa(NU - 1)

        if ti == 0:
            tap("acc", accg[:], tuple("acc%d" % gi2 for gi2 in range(NG)))

        for mc in range(8):
            bk = mc % 2
            g("tensor", [lambda e, mc=mc, kc=kc, bk=bk: e.matmul(
                bank(bk, cols=slice(0, 256)),
                wbf["wo2"][:, kc * 1024 + mc * 128: kc * 1024 + (mc + 1) * 128],
                accg[:, csl(kc)], start=(kc == 0), stop=(kc == 7)) for kc in range(8)],
              reads=tuple("acc%d" % gi2 for gi2 in range(NG)) + ("wo2",),
              writes=("p%d" % bk,))
            g("vector", [
                lambda e, mc=mc, bk=bk: e.tensor_scalar(
                    nsc[:, 0:256], bank(bk, cols=slice(0, 256)), mvc(5, mc),
                    bog2[:, mc:mc + 1], ALU.mult, ALU.add),
                lambda e, mc=mc: e.tensor_tensor(
                    xsb[:, csl(mc)], xsb[:, csl(mc)], nsc[:, 0:256], ALU.add),
            ], reads=("p%d" % bk, "modv", "bog2", "xsb%d" % mc), writes=("xsb%d" % mc,))

        g("sync", [lambda e, t0=t0: e.dma_start(
            out=oT[:, t0:t0 + 256].rearrange("(k p) t -> p k t", p=128),
            in_=xsb[:].rearrange("p (k t) -> p k t", t=256))],
          reads=tuple("xsb%d" % k for k in range(8)), dma=True)

    tap("modv", modv[:], ("modv",))
    tap("bog1", bog1[:], ("bog1",))
    if _SS < 6: return _fin(nc, P)
    for wt in range(NWT):
        run_tile(True, xw, wt * TT)
    g("vector", [lambda e: e.tensor_scalar(hst[:], hst[:], wcol[:], None, ALU.mult)],
      reads=("wcol", "hst"), writes=("hst",))
    for ti in range(NTILE):
        run_tile(False, xT, ti * TT, ti=ti)

    return _fin(nc, P)


def _fin(nc, P):
    P.q["sync"].append(("w", "X", P.cnt["X"]))
    P.q["gpsimd"].append(("w", "G", P.cnt["G"]))

    from contextlib import ExitStack
    with ExitStack() as stack:
        sems = {s: stack.enter_context(nc.semaphore("sem" + s)) for s in "TVSXG"}
        blk = stack.enter_context(nc.Block())

        def replay(name):
            sem = sems[Prog.E2S[name]]
            def runner(e):
                for item in P.q[name]:
                    if item[0] == "w":
                        e.wait_ge(sems[item[1]], item[2])
                    elif item[0] == "d":
                        item[1](e).then_inc(sem, 16)
                    elif item[0] == "i":
                        item[1](e)
                    else:
                        item[1](e).then_inc(sem, 1)
            return runner
        blk.tensor(replay("tensor"))
        blk.vector(replay("vector"))
        blk.scalar(replay("scalar"))
        blk.sync(replay("sync"))
        blk.gpsimd(replay("gpsimd"))
    return nc


_NC_CACHE = None


def kernel(**inp):
    global _NC_CACHE
    x = np.asarray(inp["L0_attn_x"], np.float32)
    ws = _host_slab(inp)
    if _NC_CACHE is None:
        _NC_CACHE = build_nc()
    nc = _NC_CACHE
    c = np.asarray(inp["L0_c"], np.float32)
    in_maps = []
    for core in range(NCORE):
        b, half = core // 2, core % 2
        t0 = half * CT
        xTc = np.ascontiguousarray(x[b, t0:t0 + CT, :].T)
        if half == 1:
            xwc = np.ascontiguousarray(x[b, t0 - WARM:t0, :].T)
            wv = 1.0
        else:
            xwc = np.zeros((D, WARM), np.float32)
            wv = 0.0
        in_maps.append({
            "xT": xTc, "xw": xwc, "wslab": ws,
            "cT": np.ascontiguousarray(c[b].reshape(8, 128).T),
            "wcol": np.full((128, 1), wv, np.float32),
        })
    res = run_bass_kernel_spmd(nc, in_maps, core_ids=list(range(NCORE)))
    out = np.empty((B, TFULL, D), np.float32)
    for core in range(NCORE):
        b, half = core // 2, core % 2
        out[b, half * CT:(half + 1) * CT, :] = res.results[core]["oT"].T
    return out


# version 28
# speedup vs baseline: 1.5214x; 1.5214x over previous
import numpy as np
import concourse.bass as bass
import concourse.mybir as mybir
from concourse.bass_utils import run_bass_kernel_spmd

F32 = mybir.dt.float32
BF16 = mybir.dt.bfloat16
AF = mybir.ActivationFunctionType
ALU = mybir.AluOpType
AX = mybir.AxisListType

D = 1024
NH = 16
DH = 64
B = 4
TFULL = 8192
NCORE = 8
CT = 4096
TT = 256
NTILE = CT // TT
WARM = 512
NWT = WARM // TT
C = 64
NCH = TT // C
NM = 8
EPS = 1e-5
NE = 16
NG = (TT * 8) // 512
_SS = 99
_TAPS = ()


def _slab_layout():
    off, o = {}, 0
    def put(name, w):
        nonlocal o
        off[name] = (o, w)
        o += w
    for nm in ("wq", "wk", "wv", "wg", "wo", "wm", "wo2"):
        put(nm, 8192)
    put("w1w2", NE * 1024)
    put("emb", 16)
    put("b2", 128)
    put("pv", 72)
    put("b1v", 64)
    put("al", 8)
    put("wmod", 8 * 6144)
    put("cq", 64)
    put("ck", 64)
    put("ck2", 64)
    put("tril", 256)
    put("i128", 128)
    return off, o


SLAB_OFF, SLAB_W = _slab_layout()


def _host_slab(inp):
    ws = np.zeros((128, SLAB_W), np.float32)
    def put(name, arr):
        o, w = SLAB_OFF[name]
        ws[:arr.shape[0], o:o + arr.shape[1]] = arr
    def fold(m):
        X = m.shape[1]
        return np.ascontiguousarray(m.reshape(8, 128, X).transpose(1, 0, 2).reshape(128, 8 * X))
    put("wq", fold(np.ascontiguousarray(np.asarray(inp["L0_attn_WQ"]).T)))
    put("wk", fold(np.ascontiguousarray(np.asarray(inp["L0_attn_WK"]).T)))
    put("wv", fold(np.ascontiguousarray(np.asarray(inp["L0_attn_WV"]).T)))
    put("wg", fold(np.ascontiguousarray(np.asarray(inp["L0_attn_Wg"]).T)))
    put("wo", fold(np.ascontiguousarray(np.asarray(inp["L0_attn_WO"]).T)))
    put("wm", fold(np.asarray(inp["L0_ffn_Wm"])))
    put("wo2", fold(np.asarray(inp["L0_ffn_Wo"])))
    w12 = np.zeros((128, NE * 1024), np.float32)
    W1 = np.asarray(inp["L0_ffn_W1"])
    W2 = np.asarray(inp["L0_ffn_W2"])
    for e in range(NE):
        w12[:, e * 1024:e * 1024 + 512] = W1[e]
        w12[:, e * 1024 + 512:(e + 1) * 1024] = \
            W2[e].reshape(4, 128, 128).transpose(1, 0, 2).reshape(128, 512)
    put("w1w2", w12)
    put("emb", np.asarray(inp["L0_ffn_expert_emb"]))
    put("b2", np.asarray(inp["L0_ffn_b2"]))
    pvh = np.zeros((128, 72), np.float32)
    for i, nm in enumerate(["L0_attn_bQ", "L0_attn_bK", "L0_attn_bV", "L0_attn_bg",
                            "L0_attn_bO", "L0_ffn_bm", "L0_ffn_bo",
                            "L0_attn_ln_gamma", "L0_attn_ln_beta"]):
        pvh[:, i * 8:(i + 1) * 8] = np.asarray(inp[nm]).reshape(8, 128).T
    put("pv", pvh)
    b1 = np.asarray(inp["L0_ffn_b1"])
    b1h = np.zeros((128, 64), np.float32)
    for e in range(NE):
        b1h[:, e * 4:(e + 1) * 4] = b1[e].reshape(4, 128).T
    put("b1v", b1h)
    al = np.asarray(inp["L0_attn_alpha_logit"])
    put("al", np.concatenate([al[0::2].T, al[1::2].T], axis=0))
    put("wmod", fold(np.asarray(inp["L0_wmod"])))
    put("cq", np.broadcast_to(np.arange(1, 65, dtype=np.float32), (128, 64)))
    put("ck", np.broadcast_to(-np.arange(1, 65, dtype=np.float32), (128, 64)))
    put("ck2", np.broadcast_to(63.0 - np.arange(64, dtype=np.float32), (128, 64)))
    tr = (np.arange(64)[:, None] <= np.arange(64)[None, :]).astype(np.float32)
    put("tril", np.tile(tr, (2, 4)))
    put("i128", np.eye(128, dtype=np.float32))
    return ws


class Prog:
    ENGS = ("tensor", "vector", "scalar", "sync", "gpsimd")
    E2S = {"tensor": "T", "vector": "V", "scalar": "S", "sync": "X", "gpsimd": "G"}

    def __init__(self):
        self.q = {e: [] for e in self.ENGS}
        self.cnt = {s: 0 for s in "TVSXG"}
        self.res = {}
        self.lastwait = {e: {} for e in self.ENGS}

    def group(self, eng, builders, reads=(), writes=(), dma=False):
        assert builders
        sem = self.E2S[eng]
        own_dma = eng in ("sync", "gpsimd")
        waits = {}
        def need(s, v):
            if s == sem and not own_dma:
                return
            if s == sem and own_dma and not dma:
                return
            waits[s] = max(waits.get(s, 0), v)
        for n in reads:
            e = self.res.get(n)
            if e and e["w"]:
                need(*e["w"])
        for n in writes:
            e = self.res.get(n)
            if e:
                if e["w"]:
                    need(*e["w"])
                for s, v in e["r"].items():
                    need(s, v)
        for s in sorted(waits):
            v = waits[s]
            lw = self.lastwait[eng]
            if lw.get(s, -1) >= v:
                continue
            lw[s] = v
            self.q[eng].append(("w", s, v))
        if dma:
            self.cnt[sem] += 16 * len(builders)
            for b in builders:
                self.q[eng].append(("d", b))
        else:
            self.cnt[sem] += 1
            for i, b in enumerate(builders):
                self.q[eng].append(("i", b) if i < len(builders) - 1 else ("f", b))
        endv = self.cnt[sem]
        for n in reads:
            e = self.res.setdefault(n, {"w": None, "r": {}})
            e["r"][sem] = endv
        for n in writes:
            self.res[n] = {"w": (sem, endv), "r": {}}
        return endv


def build_nc():
    nc = bass.Bass("TRN2", target_bir_lowering=False)

    xT = nc.dram_tensor("xT", [D, CT], F32, kind="ExternalInput").ap()
    xw = nc.dram_tensor("xw", [D, WARM], F32, kind="ExternalInput").ap()
    slab = nc.dram_tensor("wslab", [128, SLAB_W], F32, kind="ExternalInput").ap()
    cTd = nc.dram_tensor("cT", [128, 8], F32, kind="ExternalInput").ap()
    wcd = nc.dram_tensor("wcol", [128, 1], F32, kind="ExternalInput").ap()
    oT = nc.dram_tensor("oT", [D, CT], F32, kind="ExternalOutput").ap()

    A = nc.alloc_sbuf_tensor
    wbf = {nm: A("wbf_" + nm, [128, 8192], BF16)
           for nm in ("wq", "wk", "wv", "wg", "wo", "wm", "wo2")}
    embbf = A("embbf", [128, 16], BF16)
    b2bf = A("b2bf", [16, 128], BF16)
    i128b = A("i128b", [128, 128], BF16)
    i128f = A("i128f", [128, 128], F32)
    trilr = A("trilr", [128, 256], BF16)
    onescol = A("onescol", [128, 1], BF16)
    onesrow = A("onesrow", [1, 128], BF16)
    pv = A("pv", [128, 72], F32)
    b1v = A("b1v", [128, 64], F32)
    alv = A("alv", [128, 8], F32)
    lna = A("lna", [128, 8], F32)
    acl = A("acl", [128, 8], F32)
    modv = A("modv", [128, 48], F32)
    bog1 = A("bog1", [128, 8], F32)
    bog2 = A("bog2", [128, 8], F32)
    dq = A("dq", [128, 512], BF16)
    dk = A("dk", [128, 512], BF16)
    dk2t = A("dk2t", [128, 1024], BF16)
    bkd2 = A("bkd2", [128, 1024], BF16)
    bvrep = A("bvrep", [128, 1024], BF16)
    wcol = A("wcol_sb", [128, 1], F32)
    xsb = A("xsb", [128, 2048], F32)
    hbuf = A("hbuf", [128, 2048], BF16)
    xnb = A("xnb", [128, 2048], BF16)
    qb = A("qb", [128, 2048], BF16)
    kb = A("kb", [128, 2048], BF16)
    vd = A("vd", [128, 4096], BF16)
    kh = A("kh", [128, 2048], BF16)
    gbuf = A("gbuf", [128, 2048], BF16)
    atb = A("atb", [128, 512], BF16)
    hst = A("hst", [128, 512], BF16)
    accg = A("accg", [128, 2048], BF16)
    hmid = A("hmid", [128, 2048], BF16)
    mrep = A("mrep", [128, 2048], BF16)
    combt = A("combt", [128, 1024], BF16)
    mrow = A("mrow", [1, 1024], BF16)
    mfm = A("mfm", [16, 2048], BF16)
    mtm = A("mtm", [128, 256], F32)
    wsb = A("wsb", [128, 256], F32)
    smx = A("smx", [128, 64], F32)
    sqb = A("sqb", [128, 512], BF16)
    stv = A("stv", [1, 1024], F32)
    stvb = A("stvb", [1, 512], BF16)
    nsc = A("nsc", [128, 512], F32)
    webf = A("webf", [128, 2048], BF16)
    wstg = A("wstg", [128, 2048], F32)
    silc = A("silc", [128, 8], F32)
    epsc = A("epsc", [128, 1], F32)

    pb = nc.alloc_psum_tensor("pb", [128, 4096], F32)

    def bank(i, rows=None, cols=None):
        r = rows if rows is not None else slice(0, 128)
        c = cols if cols is not None else slice(0, 512)
        return pb[r, i * 512 + c.start: i * 512 + c.stop]

    P = Prog()
    g = P.group
    WH = ("wstg0", "wstg1")
    tapd = {}

    def tap(name, ap, reads):
        if name not in _TAPS:
            return
        shp = list(ap.shape)
        t = nc.dram_tensor("tap_" + name, shp, F32, kind="ExternalOutput").ap()
        tapd[name] = True
        g("gpsimd", [lambda e: e.dma_start(out=t, in_=ap)], reads=reads, dma=True)

    def sl(name, csl=None):
        o, w = SLAB_OFF[name]
        if csl is None:
            csl = slice(0, w)
        return slab[:, o + csl.start:o + csl.stop]

    g("sync", [
        lambda e: e.dma_start(out=pv[:], in_=sl("pv")),
        lambda e: e.dma_start(out=b1v[:], in_=sl("b1v")),
        lambda e: e.dma_start(out=alv[:], in_=sl("al")),
        lambda e: e.dma_start(out=wstg[:, 0:16], in_=sl("emb")),
        lambda e: e.dma_start(out=wstg[:, 16:144], in_=sl("b2")),
        lambda e: e.dma_start(out=wstg[:, 144:400], in_=sl("tril")),
        lambda e: e.dma_start(out=wstg[:, 400:528], in_=sl("i128")),
        lambda e: e.dma_start(out=wstg[:, 536:600], in_=sl("ck")),
        lambda e: e.dma_start(out=wstg[:, 600:664], in_=sl("cq")),
        lambda e: e.dma_start(out=wstg[:, 664:728], in_=sl("ck2")),
        lambda e: e.dma_start(out=silc[:], in_=cTd),
        lambda e: e.dma_start(out=wcol[:], in_=wcd),
    ], writes=WH + ("pv", "b1v", "alv", "silc", "wcol"), dma=True)
    g("vector", [
        lambda e: e.tensor_copy(embbf[:], wstg[:, 0:16]),
        lambda e: e.tensor_copy(b2bf[:], wstg[0:16, 16:144]),
        lambda e: e.tensor_copy(trilr[:], wstg[:, 144:400]),
        lambda e: e.tensor_copy(i128b[:], wstg[:, 400:528]),
        lambda e: e.tensor_copy(i128f[:], wstg[:, 400:528]),
        lambda e: e.memset(onesrow[:], 1.0),
        lambda e: e.memset(onescol[:], 1.0),
        lambda e: e.memset(epsc[:], EPS),
    ], reads=WH, writes=("embbf", "b2bf", "trilr", "i128b", "i128f", "onesrow", "onescol"))

    if _SS < 1: return _fin(nc, P)
    g("scalar", [
        lambda e: e.activation(lna[:], alv[:], AF.Sigmoid),
        lambda e: e.drain(),
        lambda e: e.activation(lna[:], lna[:], AF.Ln),
        lambda e: e.drain(),
        lambda e: e.activation(acl[:], lna[:], AF.Exp, scale=64.0),
    ], reads=("alv",), writes=("lna", "acl"))
    for src_off, dst, dnm in ((600, dq, "dq"), (536, dk, "dk")):
        g("vector", [lambda e, m=m, src_off=src_off: e.tensor_scalar(
            wstg[:, 1048 + m * 64:1048 + (m + 1) * 64],
            wstg[:, src_off:src_off + 64], lna[:, m:m + 1], None, ALU.mult)
            for m in range(NM)], reads=WH + ("lna",), writes=WH)
        g("scalar", [lambda e, dst=dst: e.activation(dst[:], wstg[:, 1048:1560], AF.Exp)],
          reads=WH, writes=(dnm,))
    if _SS < 2: return _fin(nc, P)
    g("vector", [lambda e, m=m: e.tensor_scalar(
        wstg[:, 1048 + m * 64:1048 + (m + 1) * 64],
        wstg[:, 664:728], lna[:, m:m + 1], None, ALU.mult)
        for m in range(NM)], reads=WH + ("lna",), writes=WH)
    g("tensor", [lambda e, h=h: e.transpose(
        pb[0:64, (h % 2) * 512 + (h // 2) * 64:(h % 2) * 512 + (h // 2) * 64 + 64],
        wstg[(h % 2) * 64:(h % 2) * 64 + 64,
             1048 + (h // 2) * 64:1048 + (h // 2) * 64 + 64],
        i128f[(h % 2) * 64:(h % 2) * 64 + 64, (h % 2) * 64:(h % 2) * 64 + 64])
        for h in range(16)],
      reads=WH + ("i128f",), writes=("p0", "p1"))
    g("scalar", [lambda e, hp=hp: e.activation(
        dk2t[0:64, hp * 64:1024].rearrange("p (a c) -> p a c", c=64)[:, 0:8:1, :]
        if False else
        dk2t[0:64, 0:1024].rearrange("p (a c) -> p a c", c=64)[:, hp::2, :],
        pb[0:64, hp * 512:hp * 512 + 512].rearrange("p (a c) -> p a c", c=64),
        AF.Exp) for hp in range(2)],
      reads=("p0", "p1"), writes=("dk2t",))
    g("gpsimd", [lambda e: e.dma_start(out=dk2t[64:128, :], in_=dk2t[0:64, :])],
      reads=("dk2t",), writes=("dk2t",), dma=True)

    if _SS < 3: return _fin(nc, P)
    for vi, dst in ((1, "bkd2"), (2, "bvrep")):
        g("tensor", [lambda e, kc=kc, vi=vi: e.transpose(
            pb[0:1, kc * 128:(kc + 1) * 128],
            pv[:, vi * 8 + kc:vi * 8 + kc + 1], i128f[:])
            for kc in range(8)],
          reads=("pv", "i128f", "stv"), writes=("p0", "p1"))
        g("vector", [lambda e: e.tensor_copy(stv[0:1, 0:1024], pb[0:1, 0:1024])],
          reads=("p0", "p1"), writes=("stv",))
        g("gpsimd", [
            lambda e: e.dma_start(out=wstg[:, 0:512],
                                  in_=stv[0:1, 0:512].unsqueeze(1).broadcast_to([1, 128, 512])),
            lambda e: e.dma_start(out=wstg[:, 512:1024],
                                  in_=stv[0:1, 512:1024].unsqueeze(1).broadcast_to([1, 128, 512])),
        ], reads=("stv",), writes=WH, dma=True)
        if dst == "bkd2":
            g("vector", [
                lambda e: e.tensor_copy(bkd2[:], wstg[:, 0:1024]),
                lambda e: e.tensor_tensor(bkd2[:], bkd2[:], dk2t[:], ALU.mult),
            ], reads=WH + ("dk2t",), writes=("bkd2",))
        else:
            g("vector", [lambda e: e.tensor_copy(bvrep[:], wstg[:, 0:1024])],
              reads=WH, writes=("bvrep",))

    if _SS < 4: return _fin(nc, P)
    g("scalar", [lambda e: e.activation(silc[:], silc[:], AF.Silu)],
      reads=("silc",), writes=("silc",))
    wi = 0
    for dc in range(8):
        for s3 in range(6):
            lo = dc * 6144 + s3 * 1024
            hb_ = wi % 2
            g("sync", [lambda e, lo=lo, hb_=hb_: e.dma_start(
                out=wstg[:, hb_ * 1024:(hb_ + 1) * 1024], in_=sl("wmod", slice(lo, lo + 1024)))],
              writes=("wstg%d" % hb_,), dma=True)
            g("tensor", [lambda e, dc=dc, fi=fi, fc=s3 * 8 + fi, hb_=hb_: e.matmul(
                pb[:, 2048 + fc * 8 + dc:2048 + fc * 8 + dc + 1],
                wstg[:, hb_ * 1024 + fi * 128:hb_ * 1024 + (fi + 1) * 128],
                silc[:, dc:dc + 1], start=True, stop=True)
                for fi in range(8)],
              reads=("wstg%d" % hb_, "silc"), writes=("p4",))
            wi += 1
    g("vector", [
        lambda e: e.tensor_reduce(
            modv[:].rearrange("p (a c) -> p a c", c=1),
            pb[:, 2048:2432].rearrange("p (a c) -> p a c", c=8), AX.X, ALU.add),
        lambda e: e.drain(),
        lambda e: e.tensor_scalar(modv[:, 8:16], modv[:, 8:16], 1.0, None, ALU.add),
        lambda e: e.tensor_scalar(modv[:, 32:40], modv[:, 32:40], 1.0, None, ALU.add),
    ], reads=("p4",), writes=("modv",))
    clops = []
    for j, cg in enumerate((16, 40)):
        clops.append(lambda e, cg=cg, j=j: e.tensor_scalar(
            nsc[:, j * 8:j * 8 + 8], modv[:, cg:cg + 8], 0.25, None, ALU.max))
    for j, cg in enumerate((16, 40)):
        clops.append(lambda e, cg=cg, j=j: e.tensor_scalar(
            nsc[:, 16 + j * 8:24 + j * 8], modv[:, cg:cg + 8], -0.25, None, ALU.min))
    for j, cg in enumerate((16, 40)):
        clops.append(lambda e, cg=cg, j=j: e.tensor_scalar(
            nsc[:, 32 + j * 8:40 + j * 8], modv[:, cg:cg + 8], 0.0, None, ALU.is_ge))
    for j in range(2):
        clops.append(lambda e, j=j: e.tensor_tensor(
            nsc[:, j * 8:j * 8 + 8], nsc[:, j * 8:j * 8 + 8],
            nsc[:, 16 + j * 8:24 + j * 8], ALU.subtract))
    for j in range(2):
        clops.append(lambda e, j=j: e.tensor_tensor(
            nsc[:, j * 8:j * 8 + 8], nsc[:, 32 + j * 8:40 + j * 8],
            nsc[:, j * 8:j * 8 + 8], ALU.mult))
    for j, cg in enumerate((16, 40)):
        clops.append(lambda e, cg=cg, j=j: e.tensor_tensor(
            modv[:, cg:cg + 8], nsc[:, j * 8:j * 8 + 8],
            nsc[:, 16 + j * 8:24 + j * 8], ALU.add))
    g("vector", clops, reads=("modv",), writes=("modv",))
    g("vector", [
        lambda e: e.tensor_tensor(bog1[:], pv[:, 32:40], modv[:, 16:24], ALU.mult),
        lambda e: e.tensor_tensor(bog2[:], pv[:, 48:56], modv[:, 40:48], ALU.mult),
        lambda e: e.memset(hst[:], 0.0),
    ], reads=("pv", "modv"), writes=("bog1", "bog2", "hst"))

    if _SS < 5: return _fin(nc, P)
    pi = 0
    for nm in ("wq", "wk", "wv", "wg", "wo", "wm", "wo2"):
        for part in range(8):
            lo = part * 1024
            hb_ = pi % 2
            g("sync", [lambda e, nm=nm, lo=lo, hb_=hb_: e.dma_start(
                out=wstg[:, hb_ * 1024:(hb_ + 1) * 1024], in_=sl(nm, slice(lo, lo + 1024)))],
              writes=("wstg%d" % hb_,), dma=True)
            eng = "vector" if pi % 4 < 2 else "scalar"
            if eng == "vector":
                g("vector", [lambda e, nm=nm, lo=lo, hb_=hb_: e.tensor_copy(
                    wbf[nm][:, lo:lo + 1024], wstg[:, hb_ * 1024:(hb_ + 1) * 1024])],
                  reads=("wstg%d" % hb_,), writes=(nm,))
            else:
                g("scalar", [lambda e, nm=nm, lo=lo, hb_=hb_: e.copy(
                    wbf[nm][:, lo:lo + 1024], wstg[:, hb_ * 1024:(hb_ + 1) * 1024])],
                  reads=("wstg%d" % hb_,), writes=(nm,))
            pi += 1

    st_mu, st_m2, st_r, st_t = (slice(0, 256), slice(256, 512),
                                slice(512, 768), slice(768, 1024))

    def csl(k):
        return slice(k * 256, (k + 1) * 256)

    def pvc(vi, k):
        return pv[:, vi * 8 + k: vi * 8 + k + 1]

    def mvc(blk, k):
        return modv[:, blk * 8 + k: blk * 8 + k + 1]

    def ln_phase(src, sname, dst, dname, a_col, s_col, f32src):
        for k in range(8):
            sops = []
            if f32src:
                sops.append(lambda e, k=k: e.copy(hbuf[:, csl(k)], src[:, csl(k)]))
            sops.append(lambda e, k=k: e.activation(
                sqb[:, (k % 2) * 256:(k % 2) * 256 + 256], src[:, csl(k)], AF.Square))
            wr = (("hb%d" % k,) if f32src else ()) + ("sq%d" % (k % 2),)
            g("scalar", sops, reads=("%s%d" % (sname, k),), writes=wr)
            stat = hbuf if f32src else src
            statn = ("hb%d" % k) if f32src else ("%s%d" % (sname, k))
            g("tensor", [
                lambda e, k=k, stat=stat: e.matmul(
                    pb[0:1, 2560:2816], onescol[:], stat[:, csl(k)],
                    start=(k == 0), stop=(k == 7)),
                lambda e, k=k: e.matmul(
                    pb[0:1, 3072:3328], onescol[:],
                    sqb[:, (k % 2) * 256:(k % 2) * 256 + 256],
                    start=(k == 0), stop=(k == 7)),
            ], reads=(statn, "sq%d" % (k % 2), "onescol"), writes=("p5", "p6"))
        g("vector", [
            lambda e: e.tensor_scalar(stv[:, st_mu], pb[0:1, 2560:2816], 1.0 / D, None, ALU.mult),
            lambda e: e.tensor_scalar(stv[:, st_m2], pb[0:1, 3072:3328], 1.0 / D, None, ALU.mult),
            lambda e: e.tensor_tensor(stv[:, st_t], stv[:, st_mu], stv[:, st_mu], ALU.mult),
            lambda e: e.drain(),
            lambda e: e.tensor_tensor(stv[:, st_m2], stv[:, st_m2], stv[:, st_t], ALU.subtract),
        ], reads=("p5", "p6"), writes=("stv",))
        g("scalar", [
            lambda e: e.activation(stv[:, st_t], stv[:, st_m2], AF.Ln, bias=epsc[0:1, :]),
            lambda e: e.drain(),
            lambda e: e.activation(stv[:, st_r], stv[:, st_t], AF.Exp, scale=-0.5),
        ], reads=("stv",), writes=("stv",))
        g("vector", [
            lambda e: e.tensor_tensor(stv[:, st_t], stv[:, st_mu], stv[:, st_r], ALU.mult),
            lambda e: e.tensor_copy(stvb[:, 0:256], stv[:, st_r]),
            lambda e: e.tensor_copy(stvb[:, 256:512], stv[:, st_t]),
        ], reads=("stv",), writes=("stvb",))
        g("tensor", [
            lambda e: e.matmul(pb[:, 2048:2304], onesrow[:], stvb[:, 0:256],
                               start=True, stop=True),
            lambda e: e.matmul(pb[:, 2304:2560], onesrow[:], stvb[:, 256:512],
                               start=True, stop=True),
        ], reads=("stvb", "onesrow"), writes=("p4",))
        for kh_ in range(2):
            nops = []
            for k in range(kh_ * 4, kh_ * 4 + 4):
                nops.append(lambda e, k=k: e.scalar_tensor_tensor(
                    nsc[:, 0:256], src[:, csl(k)], a_col(k), pb[:, 2048:2304],
                    ALU.mult, ALU.mult))
                nops.append(lambda e, k=k: e.scalar_tensor_tensor(
                    nsc[:, 256:512], pb[:, 2304:2560], a_col(k), nsc[:, 0:256],
                    ALU.mult, ALU.subtract))
                nops.append(lambda e, k=k: e.tensor_scalar(
                    dst[:, csl(k)], nsc[:, 256:512], -1.0, s_col(k), ALU.mult, ALU.add))
            g("vector", nops,
              reads=tuple("%s%d" % (sname, k) for k in range(kh_ * 4, kh_ * 4 + 4))
                    + ("p4", "modv", "pv"),
              writes=tuple("%s%d" % (dname, k) for k in range(kh_ * 4, kh_ * 4 + 4)))

    XN = tuple("xn%d" % k for k in range(8))

    def proj_fm(wname, evac):
        for mc in range(8):
            bk = mc % 2
            g("tensor", [lambda e, mc=mc, kc=kc, bk=bk: e.matmul(
                bank(bk, cols=slice(0, 256)),
                wbf[wname][:, kc * 1024 + mc * 128: kc * 1024 + (mc + 1) * 128],
                xnb[:, csl(kc)], start=(kc == 0), stop=False) for kc in range(4)],
              reads=XN[:4] + (wname,), writes=("p%d" % bk,))
            g("tensor", [lambda e, mc=mc, kc=kc, bk=bk: e.matmul(
                bank(bk, cols=slice(0, 256)),
                wbf[wname][:, kc * 1024 + mc * 128: kc * 1024 + (mc + 1) * 128],
                xnb[:, csl(kc)], start=False, stop=(kc == 7)) for kc in range(4, 8)],
              reads=XN[4:] + (wname,), writes=("p%d" % bk,))
            evac(mc, bk)

    qrep = lambda mc: dq[:, mc * 64:(mc + 1) * 64].unsqueeze(1).broadcast_to([128, 4, 64])
    krep = lambda mc: dk[:, mc * 64:(mc + 1) * 64].unsqueeze(1).broadcast_to([128, 4, 64])

    def run_tile(warm, xsrc, t0, ti=-1):
        g("sync", [lambda e: e.dma_start(
            out=xsb[:].rearrange("p (k t) -> p k t", t=256),
            in_=xsrc[:, t0:t0 + 256].rearrange("(k p) t -> p k t", p=128))],
          writes=tuple("xsb%d" % k for k in range(8)), dma=True)

        ln_phase(xsb, "xsb", hbuf, "hb",
                 a_col=lambda k: mvc(1, k), s_col=lambda k: mvc(0, k), f32src=True)
        if ti == 0:
            tap("h", hbuf[:], tuple("hb%d" % k for k in range(8)))
        ln_phase(hbuf, "hb", xnb, "xn",
                 a_col=lambda k: pvc(7, k), s_col=lambda k: pvc(8, k), f32src=False)
        if ti == 0:
            tap("xn", xnb[:], XN)

        for tg in range(2):
            for wname in ("wk", "wv"):
                bb = 2 if wname == "wk" else 4
                for kq in range(2):
                    g("tensor", [lambda e, kc=kc, nh_=nh_, tg=tg, wname=wname, bb=bb: e.matmul(
                        bank(bb + nh_),
                        xnb[:, kc * 256 + tg * 128: kc * 256 + (tg + 1) * 128],
                        wbf[wname][:, kc * 1024 + nh_ * 512: kc * 1024 + (nh_ + 1) * 512],
                        start=(kc == 0), stop=(kc == 7))
                        for kc in range(kq * 4, kq * 4 + 4) for nh_ in range(2)],
                      reads=XN[kq * 4:kq * 4 + 4] + (wname,),
                      writes=("p%d" % bb, "p%d" % (bb + 1)))
                if wname == "wk":
                    g("vector", [
                        lambda e, tg=tg: e.tensor_tensor(
                            kh[:, tg * 1024:(tg + 1) * 1024], pb[:, 1024:2048],
                            dk2t[:], ALU.mult),
                        lambda e, tg=tg: e.tensor_tensor(
                            kh[:, tg * 1024:(tg + 1) * 1024],
                            kh[:, tg * 1024:(tg + 1) * 1024], bkd2[:], ALU.add),
                    ], reads=("p2", "p3", "dk2t", "bkd2"), writes=("kh",))
                else:
                    c0, c1 = 2 * tg, 2 * tg + 1
                    g("vector", [
                        lambda e, c0=c0: e.tensor_tensor(
                            vd[0:64, c0 * 1024:(c0 + 1) * 1024], pb[0:64, 2048:3072],
                            bvrep[0:64, :], ALU.add),
                        lambda e, c1=c1: e.tensor_tensor(
                            vd[64:128, c1 * 1024:(c1 + 1) * 1024], pb[64:128, 2048:3072],
                            bvrep[64:128, :], ALU.add),
                    ], reads=("p4", "p5", "bvrep"), writes=("vd",))
                    g("gpsimd", [
                        lambda e, c0=c0: e.dma_start(
                            out=vd[64:128, c0 * 1024:(c0 + 1) * 1024],
                            in_=vd[0:64, c0 * 1024:(c0 + 1) * 1024]),
                        lambda e, c1=c1: e.dma_start(
                            out=vd[0:64, c1 * 1024:(c1 + 1) * 1024],
                            in_=vd[64:128, c1 * 1024:(c1 + 1) * 1024]),
                    ], reads=("vd",), writes=("vd",), dma=True)

        if not warm:
            def evq(mc, bk):
                g("vector", [lambda e, mc=mc, bk=bk: e.scalar_tensor_tensor(
                    qb[:, csl(mc)].rearrange("p (a c) -> p a c", c=64),
                    bank(bk, cols=slice(0, 256)).rearrange("p (a c) -> p a c", c=64),
                    pvc(0, mc), qrep(mc), ALU.add, ALU.mult)],
                  reads=("p%d" % bk, "pv", "dq"), writes=("qb%d" % mc,))
            proj_fm("wq", evq)

            def evk(mc, bk):
                g("vector", [lambda e, mc=mc, bk=bk: e.scalar_tensor_tensor(
                    kb[:, csl(mc)].rearrange("p (a c) -> p a c", c=64),
                    bank(bk, cols=slice(0, 256)).rearrange("p (a c) -> p a c", c=64),
                    pvc(1, mc), krep(mc), ALU.add, ALU.mult)],
                  reads=("p%d" % bk, "pv", "dk"), writes=("kb%d" % mc,))
            proj_fm("wk", evk)

            def evg(mc, bk):
                g("scalar", [lambda e, mc=mc, bk=bk: e.activation(
                    gbuf[:, csl(mc)], bank(bk, cols=slice(0, 256)), AF.Silu, bias=pvc(3, mc))],
                  reads=("p%d" % bk, "pv"), writes=("gb%d" % mc,))
            proj_fm("wg", evg)

        if ti == 0 and not warm:
            tap("q", qb[:], tuple("qb%d" % k for k in range(8)))
            tap("kt", kb[:], tuple("kb%d" % k for k in range(8)))
            tap("gpre", gbuf[:], tuple("gb%d" % k for k in range(8)))
            tap("kh", kh[:], ("kh",))
            tap("vd", vd[:], ("vd",))
            tap("hst0", hst[:], ("hst",))
        def scan_S(m):
            mo = (m % 2) * 256
            g("tensor", [lambda e, c=c, hp=hp, m=m, mo=mo: e.matmul(
                bank(2 + hp, slice(hp * 64, hp * 64 + 64),
                     slice(mo + c * 64, mo + (c + 1) * 64)),
                kb[hp * 64:hp * 64 + 64, m * 256 + c * 64: m * 256 + (c + 1) * 64],
                qb[hp * 64:hp * 64 + 64, m * 256 + c * 64: m * 256 + (c + 1) * 64],
                start=True, stop=True)
                for c in range(NCH) for hp in range(2)],
              reads=("kb%d" % m, "qb%d" % m), writes=("p2", "p3"))

        def scan_at(m):
            mo = (m % 2) * 256
            g("vector", [
                lambda e, mo=mo: e.tensor_tensor(
                    atb[0:64, mo:mo + 256], pb[0:64, 1024 + mo:1024 + mo + 256],
                    trilr[0:64, :], ALU.mult),
                lambda e, mo=mo: e.tensor_tensor(
                    atb[64:128, mo:mo + 256], pb[64:128, 1536 + mo:1536 + mo + 256],
                    trilr[64:128, :], ALU.mult),
            ], reads=("p2", "p3", "trilr"), writes=("atb%d" % (m % 2),))

        def scan_c(m, c, warm):
            mo = (m % 2) * 256
            mo2 = (m % 2) * 128
            cp = c % 2
            tmm = []
            if not warm:
                for hp in range(2):
                    tmm.append(lambda e, hp=hp, m=m, c=c, mo=mo: e.matmul(
                        bank(4 + hp, slice(hp * 64, hp * 64 + 64),
                             slice(mo + c * 64, mo + (c + 1) * 64)),
                        hst[hp * 64:hp * 64 + 64, m * 64:(m + 1) * 64],
                        qb[hp * 64:hp * 64 + 64, m * 256 + c * 64: m * 256 + (c + 1) * 64],
                        start=True, stop=False))
                for hp in range(2):
                    h = 2 * m + hp
                    tmm.append(lambda e, hp=hp, h=h, c=c, mo=mo: e.matmul(
                        bank(4 + hp, slice(hp * 64, hp * 64 + 64),
                             slice(mo + c * 64, mo + (c + 1) * 64)),
                        vd[hp * 64:hp * 64 + 64, c * 1024 + h * 64: c * 1024 + (h + 1) * 64],
                        atb[hp * 64:hp * 64 + 64, mo + c * 64:mo + (c + 1) * 64],
                        start=False, stop=True))
            for hp in range(2):
                h = 2 * m + hp
                tmm.append(lambda e, hp=hp, h=h, c=c, cp=cp, mo2=mo2: e.matmul(
                    bank(6 + cp, slice(hp * 64, hp * 64 + 64),
                         slice(mo2 + (c // 2) * 64, mo2 + (c // 2) * 64 + 64)),
                    kh[cp * 64:cp * 64 + 64, (c // 2) * 1024 + h * 64: (c // 2) * 1024 + (h + 1) * 64],
                    vd[cp * 64:cp * 64 + 64, c * 1024 + h * 64: c * 1024 + (h + 1) * 64],
                    start=True, stop=True))
            rd = ("hst", "atb%d" % (m % 2), "vd", "kh", "qb%d" % m)
            wr = (("p4", "p5") if not warm else ()) + ("p%d" % (6 + cp),)
            g("tensor", tmm, reads=rd, writes=wr)

        def scan_h(m, c):
            mo2 = (m % 2) * 128
            cp = c % 2
            hops = []
            for hp in range(2):
                hops.append(lambda e, hp=hp, m=m, c=c, cp=cp, mo2=mo2: e.scalar_tensor_tensor(
                    hst[hp * 64:hp * 64 + 64, m * 64:(m + 1) * 64],
                    hst[hp * 64:hp * 64 + 64, m * 64:(m + 1) * 64],
                    acl[hp * 64:hp * 64 + 64, m:m + 1],
                    bank(6 + cp, slice(hp * 64, hp * 64 + 64),
                         slice(mo2 + (c // 2) * 64, mo2 + (c // 2) * 64 + 64)),
                    ALU.mult, ALU.add))
            g("vector", hops, reads=("p%d" % (6 + cp), "acl"), writes=("hst",))

        def scan_og(m):
            mo = (m % 2) * 256
            g("vector", [
                lambda e, m=m, mo=mo: e.tensor_tensor(
                    gbuf[0:64, m * 256:(m + 1) * 256], pb[0:64, 2048 + mo:2048 + mo + 256],
                    gbuf[0:64, m * 256:(m + 1) * 256], ALU.mult),
                lambda e, m=m, mo=mo: e.tensor_tensor(
                    gbuf[64:128, m * 256:(m + 1) * 256], pb[64:128, 2560 + mo:2560 + mo + 256],
                    gbuf[64:128, m * 256:(m + 1) * 256], ALU.mult),
            ], reads=("p4", "p5", "gb%d" % m), writes=("gb%d" % m,))

        if not warm:
            scan_S(0)
            scan_S(1)
        for mp in range(0, NM, 2):
            if not warm:
                scan_at(mp)
                scan_at(mp + 1)
            for c in range(NCH):
                scan_c(mp, c, warm)
                scan_c(mp + 1, c, warm)
                if not warm and mp + 2 < NM:
                    if c == 0:
                        scan_S(mp + 2)
                    elif c == 1:
                        scan_S(mp + 3)
                scan_h(mp, c)
                scan_h(mp + 1, c)
            if not warm:
                scan_og(mp)
                scan_og(mp + 1)

        if warm:
            return
        if ti == 0:
            tap("og", gbuf[:], tuple("gb%d" % k for k in range(8)))
            tap("hst1", hst[:], ("hst",))

        for mc in range(8):
            bk = mc % 2
            mm = [lambda e, mc=mc, kc=kc, bk=bk: e.matmul(
                bank(bk, cols=slice(0, 256)),
                wbf["wo"][:, kc * 1024 + mc * 128: kc * 1024 + (mc + 1) * 128],
                gbuf[:, csl(kc)], start=(kc == 0), stop=False) for kc in range(8)]
            mm.append(lambda e, mc=mc, bk=bk: e.matmul(
                bank(bk, cols=slice(0, 256)), i128b[:], hbuf[:, csl(mc)],
                start=False, stop=True))
            g("tensor", mm,
              reads=tuple("gb%d" % k for k in range(8)) +
                    tuple("hb%d" % k for k in range(8)) + ("wo", "i128b"),
              writes=("p%d" % bk,))
            g("vector", [
                lambda e, mc=mc, bk=bk: e.tensor_scalar(
                    nsc[:, 0:256], bank(bk, cols=slice(0, 256)), mvc(2, mc),
                    bog1[:, mc:mc + 1], ALU.mult, ALU.add),
                lambda e, mc=mc: e.tensor_tensor(
                    xsb[:, csl(mc)], xsb[:, csl(mc)], nsc[:, 0:256], ALU.add),
            ], reads=("p%d" % bk, "modv", "bog1", "xsb%d" % mc), writes=("xsb%d" % mc,))

        if ti == 0:
            tap("x1", xsb[:], tuple("xsb%d" % k for k in range(8)))
        ln_phase(xsb, "xsb", xnb, "xn",
                 a_col=lambda k: mvc(4, k), s_col=lambda k: mvc(3, k), f32src=True)

        def evm(mc, bk):
            g("vector", [lambda e, mc=mc, bk=bk: e.tensor_scalar(
                kb[:, csl(mc)], bank(bk, cols=slice(0, 256)), pvc(5, mc), None, ALU.add)],
              reads=("p%d" % bk, "pv"), writes=("kb%d" % mc,))
        proj_fm("wm", evm)

        if ti == 0:
            tap("tb", kb[:], tuple("kb%d" % k for k in range(8)))
        NU = NE * NG
        POOL2 = ((0, 1), (4, 5))

        def mrep_issue(ei, hh):
            g("gpsimd", [lambda e, ei=ei, hh=hh: e.dma_start(
                out=mrow[:], in_=mfm[ei:ei + 1, hh * 1024:(hh + 1) * 1024])],
              reads=("mfm",), writes=("mrow",), dma=True)
            g("tensor", [
                lambda e: e.matmul(pb[:, 3072:3584], onesrow[:], mrow[0:1, 0:512],
                                   start=True, stop=True),
                lambda e: e.matmul(pb[:, 3584:4096], onesrow[:], mrow[0:1, 512:1024],
                                   start=True, stop=True),
            ], reads=("mrow", "onesrow"), writes=("p6", "p7"))
            g("vector", [
                lambda e, hh=hh: e.tensor_copy(
                    mrep[:, hh * 1024:hh * 1024 + 512], pb[:, 3072:3584]),
                lambda e, hh=hh: e.tensor_copy(
                    mrep[:, hh * 1024 + 512:hh * 1024 + 1024], pb[:, 3584:4096]),
            ], reads=("p6", "p7"), writes=("mrep%d" % hh,))

        def wfetch(ei):
            eb = ei % 2
            g("sync", [lambda e, ei=ei, eb=eb: e.dma_start(
                out=wstg[:, eb * 1024:(eb + 1) * 1024],
                in_=sl("w1w2", slice(ei * 1024, (ei + 1) * 1024)))],
              writes=("wstg%d" % eb,), dma=True)
            g("vector", [lambda e, eb=eb: e.tensor_copy(
                webf[:, eb * 1024:(eb + 1) * 1024], wstg[:, eb * 1024:(eb + 1) * 1024])],
              reads=("wstg%d" % eb,), writes=("webf%d" % eb,))

        def ffn_a(j):
            u, q = j // 2, j % 2
            ei, gi = u // NG, u % NG
            eb = ei % 2
            if gi == 1 and q == 0 and ei + 1 < NE:
                wfetch(ei + 1)
            if gi == 1 and q == 0 and ei >= 1:
                mrep_issue(ei, 1)
            if gi == 3 and q == 0 and ei + 1 < NE:
                mrep_issue(ei + 1, 0)
            b0, b1 = POOL2[j % 2]
            wof = eb * 1024 + q * 256
            tsl = slice(gi * 512, (gi + 1) * 512)
            kread = ("kb%d" % (2 * gi), "kb%d" % (2 * gi + 1))
            g("tensor", [
                lambda e, wof=wof, tsl=tsl, b0=b0: e.matmul(
                    bank(b0), webf[:, wof:wof + 128], kb[:, tsl], start=True, stop=True),
                lambda e, wof=wof, tsl=tsl, b1=b1: e.matmul(
                    bank(b1), webf[:, wof + 128:wof + 256], kb[:, tsl], start=True, stop=True),
            ], reads=kread + ("webf%d" % (ei % 2),),
              writes=("p%d" % b0, "p%d" % b1))

        def ffn_b(j):
            u, q = j // 2, j % 2
            ei = u // NG
            b0, b1 = POOL2[j % 2]
            sl0 = (j % 2) * 1024
            g("scalar", [
                lambda e, ei=ei, q=q, b0=b0, sl0=sl0: e.activation(
                    hmid[:, sl0:sl0 + 512], bank(b0), AF.Gelu,
                    bias=b1v[:, ei * 4 + 2 * q:ei * 4 + 2 * q + 1]),
                lambda e, ei=ei, q=q, b1=b1, sl0=sl0: e.activation(
                    hmid[:, sl0 + 512:sl0 + 1024], bank(b1), AF.Gelu,
                    bias=b1v[:, ei * 4 + 2 * q + 1:ei * 4 + 2 * q + 2]),
            ], reads=("p%d" % b0, "p%d" % b1, "b1v"),
              writes=("hm%d" % (2 * q), "hm%d" % (2 * q + 1)))

        def ffn_c(j):
            u, q = j // 2, j % 2
            ei = u // NG
            eo = 2 + (u % 2)
            eb = ei % 2
            wof = eb * 1024 + 512 + q * 256
            sl0 = (j % 2) * 1024
            g("tensor", [
                lambda e, wof=wof, eo=eo, sl0=sl0, q=q: e.matmul(
                    bank(eo), webf[:, wof:wof + 128], hmid[:, sl0:sl0 + 512],
                    start=(q == 0), stop=False),
                lambda e, wof=wof, eo=eo, sl0=sl0, q=q: e.matmul(
                    bank(eo), webf[:, wof + 128:wof + 256], hmid[:, sl0 + 512:sl0 + 1024],
                    start=False, stop=(q == 1)),
            ], reads=("webf%d" % eb, "hm%d" % (2 * q), "hm%d" % (2 * q + 1)),
              writes=("p%d" % eo,))

        def ffn_Fm(u):
            gi = u % NG
            eo = 2 + (u % 2)
            hh, off = gi // 2, (gi % 2) * 512
            cs = (u % 2) * 512
            g("vector", [lambda e, eo=eo, hh=hh, off=off, cs=cs: e.tensor_tensor(
                combt[:, cs:cs + 512], bank(eo),
                mrep[:, hh * 1024 + off:hh * 1024 + off + 512], ALU.mult)],
              reads=("p%d" % eo, "mrep%d" % hh), writes=("combt%d" % (u % 2),))

        def ffn_Fa(u):
            gi = u % NG
            cs = (u % 2) * 512
            g("vector", [lambda e, gi=gi, cs=cs: e.tensor_tensor(
                accg[:, gi * 512:(gi + 1) * 512], accg[:, gi * 512:(gi + 1) * 512],
                combt[:, cs:cs + 512], ALU.add)],
              reads=("combt%d" % (u % 2), "acc%d" % gi), writes=("acc%d" % gi,))

        def acc_init():
            for gi in range(NG):
                g("tensor", [lambda e, gi=gi: e.matmul(
                    pb[:, 3072:3584], b2bf[:], mfm[:, gi * 512:(gi + 1) * 512],
                    start=True, stop=True)], reads=("mfm", "b2bf"), writes=("p6",))
                g("vector", [lambda e, gi=gi: e.tensor_copy(
                    accg[:, gi * 512:(gi + 1) * 512], pb[:, 3072:3584])],
                  reads=("p6",), writes=("acc%d" % gi,))

        g("tensor", [lambda e, sc=sc: e.matmul(
            pb[:, 3072 + sc * 16:3072 + (sc + 1) * 16],
            kb[:, sc * 128:(sc + 1) * 128], embbf[:], start=True, stop=True)
            for sc in range(16)],
          reads=tuple("kb%d" % k for k in range(8)) + ("embbf",), writes=("p6",))
        wfetch(0)
        ffn_a(0)
        ffn_b(0)
        ffn_a(1)
        ffn_b(1)
        ffn_a(2)
        ffn_c(0)
        ffn_b(2)
        ffn_a(3)
        ffn_c(1)
        ffn_b(3)
        ffn_a(4)
        g("vector", [lambda e: e.tensor_reduce(
            smx[:, 0:16].rearrange("p (a c) -> p a c", c=1),
            pb[:, 3072:3328].rearrange("p (a c) -> p a c", c=16), AX.X, ALU.max),
            lambda e: e.drain()],
          reads=("p6",), writes=("smx",))
        g("vector", [lambda e, sc=sc: e.tensor_scalar(
            wsb[:, sc * 16:(sc + 1) * 16], pb[:, 3072 + sc * 16:3072 + (sc + 1) * 16],
            smx[:, sc:sc + 1], None, ALU.subtract) for sc in range(16)],
          reads=("p6", "smx"), writes=("wsbf",))
        g("scalar", [lambda e: e.activation(wsb[:], wsb[:], AF.Exp)],
          reads=("wsbf",), writes=("wsbf",))
        g("vector", [
            lambda e: e.tensor_reduce(
                smx[:, 16:32].rearrange("p (a c) -> p a c", c=1),
                wsb[:].rearrange("p (a c) -> p a c", c=16), AX.X, ALU.add),
            lambda e: e.tensor_reduce(
                smx[:, 0:16].rearrange("p (a c) -> p a c", c=1),
                wsb[:].rearrange("p (a c) -> p a c", c=16), AX.X, ALU.max),
            lambda e: e.reciprocal(smx[:, 32:48], smx[:, 16:32]),
            lambda e: e.drain(),
        ], reads=("wsbf",), writes=("smx",))
        mops = []
        for sc in range(16):
            mops.append(lambda e, sc=sc: e.tensor_scalar(
                mtm[:, sc * 16:(sc + 1) * 16], wsb[:, sc * 16:(sc + 1) * 16],
                smx[:, sc:sc + 1], 0.0, ALU.subtract, ALU.is_lt))
        for sc in range(16):
            mops.append(lambda e, sc=sc: e.tensor_tensor(
                mtm[:, sc * 16:(sc + 1) * 16], wsb[:, sc * 16:(sc + 1) * 16],
                mtm[:, sc * 16:(sc + 1) * 16], ALU.mult))
        g("vector", mops, reads=("wsbf", "smx"), writes=("mtm",))
        g("vector", [lambda e: e.drain(), lambda e: e.tensor_reduce(
            smx[:, 48:64].rearrange("p (a c) -> p a c", c=1),
            mtm[:].rearrange("p (a c) -> p a c", c=16), AX.X, ALU.max),
            lambda e: e.drain()],
          reads=("mtm",), writes=("smx",))
        mops2 = []
        for sc in range(16):
            mops2.append(lambda e, sc=sc: e.tensor_scalar(
                mtm[:, sc * 16:(sc + 1) * 16], wsb[:, sc * 16:(sc + 1) * 16],
                smx[:, 48 + sc:49 + sc], 0.0, ALU.subtract, ALU.is_ge))
        for sc in range(16):
            mops2.append(lambda e, sc=sc: e.scalar_tensor_tensor(
                mtm[:, sc * 16:(sc + 1) * 16], mtm[:, sc * 16:(sc + 1) * 16],
                smx[:, 32 + sc:33 + sc], wsb[:, sc * 16:(sc + 1) * 16],
                ALU.mult, ALU.mult))
        g("vector", mops2, reads=("wsbf", "smx"), writes=("mtm",))
        def mtrans(rnd):
            g("tensor", [lambda e, sc=sc: e.transpose(
                pb[0:16, 3584 + (sc % 4) * 128:3584 + (sc % 4) * 128 + 128],
                mtm[:, sc * 16:(sc + 1) * 16], i128f[:])
                for sc in range(rnd * 4, rnd * 4 + 4)],
              reads=("mtm", "i128f"), writes=("p7",))
            g("vector", [lambda e, rnd=rnd: e.tensor_copy(
                mfm[:, rnd * 512:(rnd + 1) * 512], pb[0:16, 3584:4096])],
              reads=("p7",), writes=("mfm",))

        if ti == 0:
            tap("mtm", mtm[:], ("mtm",))
            tap("mfm", mfm[:], ("mfm",))
        NJ = 2 * NU
        mtrans(0)
        mtrans(1)
        mtrans(2)
        mtrans(3)
        mrep_issue(0, 0)
        mrep_issue(0, 1)
        acc_init()
        ffn_c(2)
        ffn_b(4)
        ffn_Fm(0)
        for j in range(5, NJ):
            ffn_a(j)
            ffn_c(j - 2)
            if (j - 2) % 2 == 1:
                u = (j - 2) // 2
                if u >= 1:
                    ffn_Fm(u)
                if u >= 1:
                    ffn_Fa(u - 1)
            ffn_b(j)
        ffn_c(NJ - 2)
        ffn_c(NJ - 1)
        ffn_Fm(NU - 1)
        ffn_Fa(NU - 2)
        ffn_Fa(NU - 1)

        if ti == 0:
            tap("acc", accg[:], tuple("acc%d" % gi2 for gi2 in range(NG)))

        for mc in range(8):
            bk = mc % 2
            g("tensor", [lambda e, mc=mc, kc=kc, bk=bk: e.matmul(
                bank(bk, cols=slice(0, 256)),
                wbf["wo2"][:, kc * 1024 + mc * 128: kc * 1024 + (mc + 1) * 128],
                accg[:, csl(kc)], start=(kc == 0), stop=(kc == 7)) for kc in range(8)],
              reads=tuple("acc%d" % gi2 for gi2 in range(NG)) + ("wo2",),
              writes=("p%d" % bk,))
            g("vector", [
                lambda e, mc=mc, bk=bk: e.tensor_scalar(
                    nsc[:, 0:256], bank(bk, cols=slice(0, 256)), mvc(5, mc),
                    bog2[:, mc:mc + 1], ALU.mult, ALU.add),
                lambda e, mc=mc: e.tensor_tensor(
                    xsb[:, csl(mc)], xsb[:, csl(mc)], nsc[:, 0:256], ALU.add),
            ], reads=("p%d" % bk, "modv", "bog2", "xsb%d" % mc), writes=("xsb%d" % mc,))

        g("sync", [lambda e, t0=t0: e.dma_start(
            out=oT[:, t0:t0 + 256].rearrange("(k p) t -> p k t", p=128),
            in_=xsb[:].rearrange("p (k t) -> p k t", t=256))],
          reads=tuple("xsb%d" % k for k in range(8)), dma=True)

    tap("modv", modv[:], ("modv",))
    tap("bog1", bog1[:], ("bog1",))
    if _SS < 6: return _fin(nc, P)
    for wt in range(NWT):
        run_tile(True, xw, wt * TT)
    g("vector", [lambda e: e.tensor_scalar(hst[:], hst[:], wcol[:], None, ALU.mult)],
      reads=("wcol", "hst"), writes=("hst",))
    for ti in range(NTILE):
        run_tile(False, xT, ti * TT, ti=ti)

    return _fin(nc, P)


def _fin(nc, P):
    P.q["sync"].append(("w", "X", P.cnt["X"]))
    P.q["gpsimd"].append(("w", "G", P.cnt["G"]))

    from contextlib import ExitStack
    with ExitStack() as stack:
        sems = {s: stack.enter_context(nc.semaphore("sem" + s)) for s in "TVSXG"}
        blk = stack.enter_context(nc.Block())

        def replay(name):
            sem = sems[Prog.E2S[name]]
            def runner(e):
                for item in P.q[name]:
                    if item[0] == "w":
                        e.wait_ge(sems[item[1]], item[2])
                    elif item[0] == "d":
                        item[1](e).then_inc(sem, 16)
                    elif item[0] == "i":
                        item[1](e)
                    else:
                        item[1](e).then_inc(sem, 1)
            return runner
        blk.tensor(replay("tensor"))
        blk.vector(replay("vector"))
        blk.scalar(replay("scalar"))
        blk.sync(replay("sync"))
        blk.gpsimd(replay("gpsimd"))
    return nc


_NC_CACHE = None


def kernel(**inp):
    global _NC_CACHE
    x = np.asarray(inp["L0_attn_x"], np.float32)
    ws = _host_slab(inp)
    if _NC_CACHE is None:
        _NC_CACHE = build_nc()
    nc = _NC_CACHE
    c = np.asarray(inp["L0_c"], np.float32)
    in_maps = []
    for core in range(NCORE):
        b, half = core // 2, core % 2
        t0 = half * CT
        xTc = np.ascontiguousarray(x[b, t0:t0 + CT, :].T)
        if half == 1:
            xwc = np.ascontiguousarray(x[b, t0 - WARM:t0, :].T)
            wv = 1.0
        else:
            xwc = np.zeros((D, WARM), np.float32)
            wv = 0.0
        in_maps.append({
            "xT": xTc, "xw": xwc, "wslab": ws,
            "cT": np.ascontiguousarray(c[b].reshape(8, 128).T),
            "wcol": np.full((128, 1), wv, np.float32),
        })
    res = run_bass_kernel_spmd(nc, in_maps, core_ids=list(range(NCORE)))
    out = np.empty((B, TFULL, D), np.float32)
    for core in range(NCORE):
        b, half = core // 2, core % 2
        out[b, half * CT:(half + 1) * CT, :] = res.results[core]["oT"].T
    return out
